# version 04
# baseline (speedup 1.0000x reference)
import numpy as np
import ml_dtypes
import concourse.bass as bass
import concourse.mybir as mybir
from concourse.bass_utils import run_bass_kernel_spmd

F32 = mybir.dt.float32
FP8 = mybir.dt.float8e4
FP16 = mybir.dt.float16
OP = mybir.AluOpType
AF = mybir.ActivationFunctionType

B, T, C, H = 8, 8192, 1024, 4096
NCH = C // 128
NHH = H // 128
LN_EPS = 1e-5
NEG_BIG = -1e38
TO = 512
TOC = TO // 128
V_, S_, T_, SY, GP = "vector", "scalar", "tensor", "sync", "gpsimd"


class Sched:
    def __init__(self):
        self.items = []
        self.val = {}

    def emit(self, eng, fn, wait=(), inc=None, inc_n=1):
        w = [t for t in wait if t is not None]
        tok = None
        if inc is not None:
            self.val[inc] = self.val.get(inc, 0) + inc_n
            tok = (inc, self.val[inc])
        self.items.append((eng, fn, w, (inc, inc_n) if inc else None))
        return tok

    def replay(self, engine_name, sems, eng):
        last_wait = {}
        for (e, fn, waits, inc) in self.items:
            if e != engine_name:
                continue
            for (sem, val) in waits:
                if last_wait.get(sem, -1) >= val:
                    continue
                last_wait[sem] = val
                eng.wait_ge(sems[sem], val)
            ins = fn()
            if inc is not None and ins is not None:
                ins.then_inc(sems[inc[0]], inc[1])


def build_program(n_tiles=16, dbg=False):
    nc = bass.Bass("TRN2", target_bir_lowering=False,
                   detect_race_conditions=False)

    def ein(name, shape, dt=F32):
        return nc.dram_tensor(name, shape, dt, kind="ExternalInput").ap()

    x_in = ein("x", [T, C])
    Wk = ein("Wk", [C, C]); Wv = ein("Wv", [C, C]); Wr = ein("Wr", [C, C])
    Wck = ein("Wck", [C, H]); Wcv = ein("Wcv", [H, C]); Wcr = ein("Wcr", [C, C])
    w1 = ein("w1", [H, C]); w2 = ein("w2", [C, H])
    tdec = ein("tdec", [128, NCH]); tfirst = ein("tfirst", [128, NCH])
    tshift = ein("tshift", [128, NCH])
    ln1g = ein("ln1g", [128, NCH]); ln1b = ein("ln1b", [128, NCH])
    ln2g = ein("ln2g", [128, NCH]); ln2b = ein("ln2b", [128, NCH])
    b1v = ein("b1v", [128, NHH]); b2v = ein("b2v", [128, NCH])
    ident = ein("ident", [128, 128])
    onesc = ein("onesc", [128, 1])
    onesr = ein("onesr", [1, 128])
    nramp = ein("nramp", [128, T])

    y_out = nc.dram_tensor("y", [T, C], F32, kind="ExternalOutput").ap()

    def scr(name, shape, dt=F32):
        kind = "ExternalOutput" if dbg else "Internal"
        return nc.dram_tensor(name, shape, dt, kind=kind).ap()

    w1T = scr("w1T", [C, H])
    wqT = scr("wqT", [H, C], FP8)
    s_xn = scr("s_xn", [n_tiles, 128, NCH, TO])
    s_acc = scr("s_acc", [n_tiles * TO, C])
    s_v = scr("s_v", [n_tiles, 128, NCH, TO])
    s_rp = scr("s_rp", [n_tiles, 128, NCH, TO])
    s_x1 = scr("s_x1", [n_tiles, 128, NCH, TO])
    s_q1 = scr("s_q1", [n_tiles, 128, NCH, TO])

    sb_used = [0]
    def sb(name, cols, dt=F32):
        t = nc.alloc_sbuf_tensor(name, [128, cols], dt)
        sb_used[0] += cols * mybir.dt.size(dt)
        return t

    wcol = sb("wcol", NCH); ucol = sb("ucol", NCH); tscol = sb("tscol", NCH)
    g1col = sb("g1col", NCH); b1col = sb("b1col", NCH)
    g2col = sb("g2col", NCH); b2col = sb("b2col", NCH)
    fb1 = sb("fb1", NHH); bqcol = sb("bqcol", NCH); b2ft = sb("b2ft", NCH)
    identt = sb("identt", 128)
    onesct = nc.alloc_sbuf_tensor("onesct", [128, 1], F32)
    onesrt = nc.alloc_sbuf_tensor("onesrt", [1, 128], F32)
    sb_used[0] += 2 * 128 * 4
    spA = sb("spA", NCH); spB = sb("spB", NCH)
    mzC = sb("mzC", NCH); khC = sb("khC", NCH)
    rcar = sb("rcar", NCH); hcar = sb("hcar", NCH); h2car = sb("h2car", NCH)
    dcs = sb("dcs", NCH); lps = sb("lps", NCH)
    b1s = sb("b1s", NCH); nus = sb("nus", NCH)
    tmp8a = sb("tmp8a", NCH); tmp8b = sb("tmp8b", NCH)
    junk = sb("junk", 8)
    stat = sb("stat", 64)
    stg = sb("stg", 128)
    nrp = sb("nrp", TO)
    bcm = sb("bcm", TO); bcr = sb("bcr", TO)
    srow = nc.alloc_sbuf_tensor("srow", [1, 3 * TO], F32)
    sb_used[0] += 3 * TO * 4

    AH = sb("AH", NCH * (TO + 1))
    AKH = sb("AKH", NCH * TO)
    AAB = sb("AAB", NCH * TO)
    AABB = sb("AABB", NCH * TO)
    AZ = sb("AZ", NCH * (TO + 1))
    AZB = sb("AZB", NCH * (TO + 1))
    AKK = sb("AKK", 2 * NCH * TO)
    AWA = sb("AWA", NCH * C)
    AX = sb("AX", C)
    ASQ = sb("ASQ", C)
    ABN = sb("ABN", 2 * C)
    AVC = sb("AVC", C)
    APC = sb("APC", C)
    FQF = sb("FQF", NCH * TO, FP16)
    assert sb_used[0] <= 221 * 1024, f"SBUF over: {sb_used[0]/1024:.1f} KiB/part"

    psA = nc.alloc_psum_tensor("psA", [128, 512], F32)
    psB = nc.alloc_psum_tensor("psB", [128, 512], F32)
    psT = nc.alloc_psum_tensor("psT", [128, 512], F32)
    psU = nc.alloc_psum_tensor("psU", [128, 512], F32)
    psS = nc.alloc_psum_tensor("psS", [128, 512], F32)
    psV = nc.alloc_psum_tensor("psV", [128, 512], F32)
    psW = nc.alloc_psum_tensor("psW", [128, 512], F32)

    S = Sched()
    vec, sca, ten = nc.vector, nc.scalar, nc.tensor
    LU = {}

    NLANES_LD, NLANES_ST = 12, 6
    lane_state = {"ld": 0, "st": 0}

    def ld(dst_ap, src_ap, wait=()):
        lane = lane_state["ld"]; lane_state["ld"] = (lane + 1) % NLANES_LD
        sem = f"sLD{lane}"
        prev = S.val.get(sem, 0)
        return S.emit(SY, lambda d=dst_ap, s=src_ap: nc.sync.dma_start(out=d, in_=s),
                      wait=list(wait) + [(sem, prev)], inc=sem, inc_n=16)

    def st(dst_ap, src_ap, wait=()):
        lane = lane_state["st"]; lane_state["st"] = (lane + 1) % NLANES_ST
        sem = f"sST{lane}"
        prev = S.val.get(sem, 0)
        return S.emit(GP, lambda d=dst_ap, s=src_ap: nc.gpsimd.dma_start(out=d, in_=s),
                      wait=list(wait) + [(sem, prev)], inc=sem, inc_n=16)

    def v(fn, wait=(), inc="sV"):
        return S.emit(V_, fn, wait=wait, inc=inc)

    def s(fn, wait=(), inc="sS"):
        return S.emit(S_, fn, wait=wait, inc=inc)

    def t(fn, wait=(), inc="sT"):
        return S.emit(T_, fn, wait=wait, inc=inc)

    def vsp():
        S.emit(V_, lambda: vec.tensor_scalar_mul(junk.ap(), junk.ap(), 1.0),
               wait=(), inc=None)

    def sigc(dst_ap, src_ap, tmp_ap, wait_first=(), wait_last=()):
        c1 = v(lambda: vec.tensor_single_scalar(tmp_ap, src_ap, 40.0, OP.min),
               wait=list(wait_first))
        s(lambda: sca.activation(tmp_ap, tmp_ap, AF.Exp), wait=[c1], inc=None)
        c3 = s(lambda: sca.activation(tmp_ap, tmp_ap, AF.Ln, bias=1.0))
        v(lambda: vec.tensor_tensor(tmp_ap, tmp_ap, src_ap, OP.max),
          wait=[c3], inc=None)
        c5 = v(lambda: vec.tensor_tensor(tmp_ap, src_ap, tmp_ap, OP.subtract))
        c6 = s(lambda: sca.activation(dst_ap, tmp_ap, AF.Exp),
               wait=[c5] + list(wait_last))
        return c6, c5

    def abn_wait():
        return [LU.get("bnc0"), LU.get("bnc1"), LU.get(("qch", 0)),
                LU.get(("qch", 1))]

    def abn_set(tok):
        LU["bnc0"] = tok; LU["bnc1"] = tok
        LU[("qch", 0)] = tok; LU[("qch", 1)] = tok

    hb3 = AH.ap().rearrange("p (c t) -> p c t", c=NCH)
    kh3 = AKH.ap().rearrange("p (c t) -> p c t", c=NCH)
    abP = [AAB.ap().rearrange("p (c t) -> p c t", c=NCH),
           AABB.ap().rearrange("p (c t) -> p c t", c=NCH)]
    zfP = [AZ.ap().rearrange("p (c t) -> p c t", c=NCH),
           AZB.ap().rearrange("p (c t) -> p c t", c=NCH)]
    hq3 = AKK.ap().bitcast(FP8)[:, 0:NHH * TO].rearrange("p (c t) -> p c t", c=NHH)
    rb3 = AKK.ap()[:, 0:NCH * (TO + 1)].rearrange("p (c t) -> p c t", c=NCH)
    kv3 = AKK.ap()[:, NCH * TO:2 * NCH * TO].rearrange("p (c t) -> p c t", c=NCH)
    wcv4 = AKK.ap()[:, 0:4 * C].rearrange("p (h c) -> p h c", h=4)
    kkg3 = ABN.ap().rearrange("p (h t) -> p h t", h=4)
    fq3 = FQF.ap().rearrange("p (c t) -> p c t", c=NCH)
    xnf3 = AH.ap()[:, 0:NCH * TO].rearrange("p (c t) -> p c t", c=NCH)
    wS3 = [AWA.ap()[:, 0:NCH * TO].rearrange("p (c t) -> p c t", c=NCH),
           AWA.ap()[:, NCH * TO:2 * NCH * TO].rearrange("p (c t) -> p c t", c=NCH)]
    wck3 = AVC.ap().rearrange("p (dh q) -> p dh q", q=128)
    vch_h = [AVC.ap()[:, 0:TO], AVC.ap()[:, TO:2 * TO]]
    pch_h = [APC.ap()[:, 0:TO], APC.ap()[:, TO:2 * TO]]
    qch_h = [ABN.ap()[:, C:C + TO], ABN.ap()[:, C + TO:2 * C]]
    bnc_h = [ABN.ap()[:, 0:C], ABN.ap()[:, C:2 * C]]
    xch_h = [AX.ap()[:, 0:C], AX.ap()[:, 0:C]]
    pch8 = APC.ap().bitcast(FP8)[:, 0:H]

    v(lambda: vec.memset(junk.ap(), 0.0), inc=None)
    tok_id = ld(identt.ap(), ident)
    tok_oc = ld(onesct.ap(), onesc)
    tok_or = ld(onesrt.ap(), onesr)
    cvl = {}
    for (nm, dst, src) in (("w", wcol, tdec), ("u", ucol, tfirst), ("ts", tscol, tshift),
                           ("g1", g1col, ln1g), ("b1", b1col, ln1b),
                           ("g2", g2col, ln2g), ("b2", b2col, ln2b)):
        cvl[nm] = ld(dst.ap(), src)
    tok_b1 = ld(fb1.ap(), b1v)
    tok_b2f = ld(b2ft.ap(), b2v)
    c1 = v(lambda: vec.tensor_copy(tmp8a.ap().bitcast(FP16)[:, 0:NCH], b2ft.ap()),
           wait=[tok_b2f])
    vsp()
    tok_bq = v(lambda: vec.tensor_copy(bqcol.ap(), tmp8a.ap().bitcast(FP16)[:, 0:NCH]),
               wait=[c1])

    pp_toks = []
    pp_tok = None
    for hb in range(NHH):
        wload = ld(AX.ap()[:, 0:C], w1.rearrange("(hb p) c -> hb p c", p=128)[hb],
                   wait=[LU.get("AX"), pp_tok])
        ev = None
        for dc in range(NCH):
            pb = psT if dc % 2 == 0 else psV
            pbk = "psT" if dc % 2 == 0 else "psV"
            tk = t(lambda dc=dc, pb=pb: ten.transpose(
                pb.ap()[:, 0:128],
                AX.ap()[:, 128 * dc:128 * dc + 128], identt.ap()),
                wait=[wload, tok_id, LU.get(pbk)])
            ev = v(lambda dc=dc, pb=pb: vec.tensor_copy(
                ABN.ap()[:, 128 * dc:128 * dc + 128], pb.ap()[:, 0:128]),
                wait=[tk, LU.get("bnc0")])
            LU[pbk] = ev
        LU["AX"] = ev
        pp_tok = st(w1T.rearrange("(cb p) h -> p cb h", p=128)[:, :, 128 * hb:128 * hb + 128],
                    ABN.ap()[:, 0:C].rearrange("p (cb q) -> p cb q", q=128),
                    wait=[ev])
        pp_toks.append(pp_tok)
        LU["bnc0"] = pp_tok
        LU[("xch", 0)] = ev
        LU[("xch", 1)] = ev
    for cb in range(NCH):
        wload = ld(AWA.ap()[:, 0:H], w2.rearrange("(cb p) h -> cb p h", p=128)[cb],
                   wait=[LU.get(("AWS", 0)), LU.get(("AWS", 1)), pp_tok])
        ev = None
        for hb in range(NHH):
            pb = psU if hb % 2 == 0 else psW
            pbk = "psU" if hb % 2 == 0 else "psW"
            tk = t(lambda hb=hb, pb=pb: ten.transpose(
                pb.ap()[:, 0:128],
                AWA.ap()[:, 128 * hb:128 * hb + 128], identt.ap()),
                wait=[wload, tok_id, LU.get(pbk)])
            ev = v(lambda hb=hb, pb=pb: vec.tensor_copy(
                ABN.ap().bitcast(FP8)[:, 128 * hb:128 * hb + 128],
                pb.ap()[:, 0:128]),
                wait=[tk, LU.get("bnc0")])
            LU[pbk] = ev
        LU[("AWS", 0)] = ev
        LU[("AWS", 1)] = ev
        pp_tok = st(wqT.rearrange("(hb p) c -> p hb c", p=128)[:, :, 128 * cb:128 * cb + 128],
                    ABN.ap().bitcast(FP8)[:, 0:H].rearrange("p (hb q) -> p hb q", q=128),
                    wait=[ev])
        pp_toks.append(pp_tok)
        LU["bnc0"] = pp_tok

    v(lambda: vec.memset(spA.ap(), NEG_BIG), wait=[tok_bq])
    v(lambda: vec.memset(spB.ap(), NEG_BIG))
    v(lambda: vec.memset(mzC.ap(), 0.0))
    v(lambda: vec.memset(khC.ap(), 0.0))
    v(lambda: vec.memset(tmp8b.ap(), 0.0))
    v(lambda: vec.memset(rcar.ap(), 0.0))
    v(lambda: vec.tensor_copy(hcar.ap(), tscol.ap()), wait=[cvl["ts"]])
    init_tok = v(lambda: vec.tensor_copy(h2car.ap(), tscol.ap()))
    init_all = [init_tok, tok_b1, tok_oc, tok_or, tok_id] + list(cvl.values())

    x3 = x_in.rearrange("(n p) c -> n p c", p=128)
    acc3 = s_acc.rearrange("(n p) c -> n p c", p=128)
    y3 = y_out.rearrange("(n p) c -> n p c", p=128)

    state = {"first": True, "final_tok": None}

    def A_items(i, TD):
        TD["xn_toks"] = []
        TD["acc_toks"] = []
        items = []

        def mk_ch(ch):
            def go():
                g = i * TOC + ch
                half = ch % 2
                xs = xch_h[half]
                tok_x = ld(xs, x3[g], wait=[LU.get(("xch", 0)), LU.get(("xch", 1))])
                c0 = 16 * ch
                st_ = stat.ap()
                v(lambda xs=xs, c0=c0, st_=st_: vec.reduce_sum(
                    st_[:, c0:c0 + 1], xs, axis=mybir.AxisListType.X),
                    wait=[tok_x], inc=None)
                t_sq = s(lambda xs=xs: sca.activation(ASQ.ap(), xs, AF.Square),
                         wait=[tok_x, LU.get("ASQ")])
                v(lambda c0=c0, st_=st_: vec.reduce_sum(
                    st_[:, c0 + 1:c0 + 2], ASQ.ap(), axis=mybir.AxisListType.X),
                    wait=[t_sq], inc=None)
                v(lambda c0=c0, st_=st_: vec.tensor_scalar_mul(
                    st_[:, c0:c0 + 1], st_[:, c0:c0 + 1], 1.0 / C), inc=None)
                v(lambda c0=c0, st_=st_: vec.tensor_scalar_mul(
                    st_[:, c0 + 1:c0 + 2], st_[:, c0 + 1:c0 + 2], 1.0 / C), inc=None)
                v(lambda c0=c0, st_=st_: vec.tensor_tensor(
                    st_[:, c0 + 2:c0 + 3], st_[:, c0:c0 + 1], st_[:, c0:c0 + 1],
                    OP.mult), inc=None)
                vsp()
                v(lambda c0=c0, st_=st_: vec.tensor_tensor(
                    st_[:, c0 + 2:c0 + 3], st_[:, c0 + 1:c0 + 2], st_[:, c0 + 2:c0 + 3],
                    OP.subtract), inc=None)
                vsp()
                t_var = v(lambda c0=c0, st_=st_: vec.tensor_single_scalar(
                    st_[:, c0 + 2:c0 + 3], st_[:, c0 + 2:c0 + 3], LN_EPS, OP.add))
                t_ln = s(lambda c0=c0, st_=st_: sca.activation(
                    st_[:, c0 + 2:c0 + 3], st_[:, c0 + 2:c0 + 3], AF.Ln),
                    wait=[t_var])
                t_rs = s(lambda c0=c0, st_=st_: sca.activation(
                    st_[:, c0 + 2:c0 + 3], st_[:, c0 + 2:c0 + 3], AF.Exp, scale=-0.5),
                    wait=[t_ln])
                t_xn = v(lambda xs=xs, c0=c0, st_=st_: vec.tensor_scalar(
                    ASQ.ap(), xs, st_[:, c0:c0 + 1], st_[:, c0 + 2:c0 + 3],
                    OP.subtract, OP.mult), wait=[t_rs])
                tk_acc = st(acc3[g], xs, wait=[t_xn, LU.get(("xch", half))])
                TD["acc_toks"].append(tk_acc)
                LU["bnc0"] = tk_acc
                v(lambda c0=c0, st_=st_: vec.reduce_sum(
                    st_[:, c0 + 3:c0 + 4], ASQ.ap(), axis=mybir.AxisListType.X),
                    wait=[t_xn], inc=None)
                t_sq2 = s(lambda: sca.activation(bnc_h[1], ASQ.ap(), AF.Square),
                          wait=[t_xn, LU.get("bnc1"), LU.get(("qch", 0)),
                                LU.get(("qch", 1))])
                t_sxx2 = v(lambda c0=c0, st_=st_: vec.reduce_sum(
                    st_[:, c0 + 4:c0 + 5], bnc_h[1], axis=mybir.AxisListType.X),
                    wait=[t_sq2])
                LU["bnc1"] = t_sxx2
                LU[("qch", 0)] = t_sxx2
                LU[("qch", 1)] = t_sxx2
                v(lambda c0=c0, st_=st_: vec.tensor_scalar_mul(
                    st_[:, c0 + 3:c0 + 4], st_[:, c0 + 3:c0 + 4], 1.0 / C), inc=None)
                v(lambda c0=c0, st_=st_: vec.tensor_scalar_mul(
                    st_[:, c0 + 4:c0 + 5], st_[:, c0 + 4:c0 + 5], 1.0 / C), inc=None)
                v(lambda c0=c0, st_=st_: vec.tensor_tensor(
                    st_[:, c0 + 5:c0 + 6], st_[:, c0 + 3:c0 + 4], st_[:, c0 + 3:c0 + 4],
                    OP.mult), inc=None)
                vsp()
                v(lambda c0=c0, st_=st_: vec.tensor_tensor(
                    st_[:, c0 + 5:c0 + 6], st_[:, c0 + 4:c0 + 5], st_[:, c0 + 5:c0 + 6],
                    OP.subtract), inc=None)
                vsp()
                t_var2 = v(lambda c0=c0, st_=st_: vec.tensor_single_scalar(
                    st_[:, c0 + 5:c0 + 6], st_[:, c0 + 5:c0 + 6], LN_EPS, OP.add))
                t_ln2 = s(lambda c0=c0, st_=st_: sca.activation(
                    st_[:, c0 + 5:c0 + 6], st_[:, c0 + 5:c0 + 6], AF.Ln),
                    wait=[t_var2])
                t_rs2 = s(lambda c0=c0, st_=st_: sca.activation(
                    st_[:, c0 + 5:c0 + 6], st_[:, c0 + 5:c0 + 6], AF.Exp,
                    scale=-0.5), wait=[t_ln2])
                t_h = v(lambda xs=xs, c0=c0, st_=st_: vec.tensor_scalar(
                    xs, ASQ.ap(), st_[:, c0 + 3:c0 + 4], st_[:, c0 + 5:c0 + 6],
                    OP.subtract, OP.mult), wait=[t_rs2, tk_acc])
                ev_xn = None
                tk1_last = None
                hlast = None
                for j in range(NCH):
                    pb1 = psT if j % 2 == 0 else psV
                    pk1 = "psT" if j % 2 == 0 else "psV"
                    pb2 = psU if j % 2 == 0 else psW
                    pk2 = "psU" if j % 2 == 0 else "psW"
                    tk1 = t(lambda j=j, pb1=pb1: ten.transpose(
                        pb1.ap()[:, 0:128],
                        ASQ.ap()[:, 128 * j:128 * j + 128], identt.ap()),
                        wait=[t_xn, tok_id, LU.get(pk1)])
                    tk1_last = tk1
                    ev_xn = v(lambda j=j, pb1=pb1: vec.tensor_copy(
                        ABN.ap()[:, 128 * j:128 * j + 128],
                        pb1.ap()[:, 0:128]),
                        wait=[tk1, LU.get("bnc0")])
                    LU[pk1] = ev_xn
                    tk2 = t(lambda j=j, pb2=pb2, xs=xs: ten.transpose(
                        pb2.ap()[:, 0:128],
                        xs[:, 128 * j:128 * j + 128], identt.ap()),
                        wait=[t_h, tok_id, LU.get(pk2)])
                    hlast = v(lambda j=j, pb2=pb2, ch=ch: vec.tensor_scalar(
                        hb3[:, j, 1 + 128 * ch:1 + 128 * ch + 128],
                        pb2.ap()[:, 0:128],
                        g1col.ap()[:, j:j + 1], b1col.ap()[:, j:j + 1],
                        OP.mult, OP.add),
                        wait=[tk2, LU.get("AH")] + init_all)
                    LU[pk2] = hlast
                LU[("xch", half)] = hlast
                tk_xn = st(s_xn[i, :, :, 128 * ch:128 * ch + 128],
                           ABN.ap()[:, 0:C].rearrange("p (c q) -> p c q", c=NCH),
                           wait=[ev_xn])
                TD["xn_toks"].append(tk_xn)
                LU["bnc0"] = tk_xn
                LU["ASQ"] = tk1_last
                TD["_hlast"] = hlast
            return go
        for ch in range(TOC):
            items.append(mk_ch(ch))

        def it_lead():
            t_lead = v(lambda: vec.tensor_copy(hb3[:, :, 0:1].squeeze(2), hcar.ap()),
                       wait=[LU.get("AH")] + init_all)
            t_car = v(lambda: vec.tensor_copy(hcar.ap(), hb3[:, :, TO:TO + 1].squeeze(2)),
                      wait=[t_lead])
            TD["hready"] = [TD["_hlast"], t_lead, t_car]
        items.append(it_lead)
        return items

    def B_items(i, TD):
        TD["vr_toks"] = []
        items = []
        tbase = i * TO

        def mk_wh(idx, kind, Wsrc, half):
            def go():
                if kind == "k" and half == 0:
                    TD["tok_nr"] = ld(nrp.ap(), nramp[:, tbase:tbase + TO],
                                      wait=[LU.get("nrp")])
                sl = (2 * idx + half) % 2
                wtok = ld(wS3[sl], Wsrc.rearrange("(dh p) c -> p dh c", p=128)
                          [:, :, 512 * half:512 * half + 512],
                          wait=[LU.get(("AWS", sl))])
                mm_last = None
                for mi in range(4 * half, 4 * half + 4):
                    ps = psA if mi % 2 == 0 else psB
                    bank = "psA" if mi % 2 == 0 else "psB"
                    mm = None
                    for ki in range(NCH):
                        mm = t(lambda ki=ki, mi=mi, ps=ps, sl=sl, half=half:
                               ten.matmul(
                                   ps.ap(),
                                   wS3[sl][:, ki, 128 * (mi - 4 * half):
                                           128 * (mi - 4 * half) + 128],
                                   hb3[:, ki, 0:TO],
                                   start=(ki == 0), stop=(ki == NCH - 1)),
                               wait=[wtok, LU.get(bank)]
                               + (TD["hready"] if ki == 0 else []))
                    mm_last = mm
                    if kind == "k":
                        ev = v(lambda mi=mi, ps=ps: vec.scalar_tensor_tensor(
                            kh3[:, mi, 0:TO], nrp.ap(),
                            wcol.ap()[:, mi:mi + 1], ps.ap(), OP.mult, OP.add),
                            wait=[mm, TD["tok_nr"], LU.get("AKH")])
                        TD["kh_ready"] = ev
                    else:
                        bh = mi % 2
                        ev = v(lambda ps=ps, bh=bh: vec.tensor_copy(
                            bnc_h[bh][:, 0:TO], ps.ap()),
                            wait=[mm, LU.get("bnc" + str(bh))])
                        dst = s_v if kind == "v" else s_rp
                        tk2 = st(dst[i, :, mi, :], bnc_h[bh][:, 0:TO], wait=[ev])
                        TD["vr_toks"].append(tk2)
                        LU["bnc" + str(bh)] = tk2
                    LU[bank] = ev
                LU[("AWS", sl)] = mm_last
                LU["AH"] = mm_last
                if kind == "k" and half == 1:
                    LU["nrp"] = TD["kh_ready"]
            return go
        for idx, (kind, Wsrc) in enumerate([("k", Wk), ("v", Wv), ("r", Wr)]):
            for half in range(2):
                items.append(mk_wh(idx, kind, Wsrc, half))
        return items

    def stage_C(i, TD, interleave=()):
        q = i % 2
        zb3 = zfP[q]
        ab3 = abP[q]
        kh_ready = TD["kh_ready"]
        inter = list(interleave)
        if state["first"]:
            v(lambda: vec.memset(zb3[:, :, 0:1].squeeze(2), NEG_BIG),
              wait=[kh_ready, LU.get(("AZ", q))], inc=None)
            vsp()
        else:
            v(lambda: vec.tensor_tensor(
                tmp8b.ap(), khC.ap(), kh3[:, :, 0:1].squeeze(2), OP.subtract),
                wait=[kh_ready], inc=None)
            vsp()
            v(lambda: vec.tensor_tensor(tmp8a.ap(), tmp8b.ap(), mzC.ap(),
                                        OP.subtract), inc=None)
            vsp()
            v(lambda: vec.scalar_tensor_tensor(
                zb3[:, :, 0:1].squeeze(2), spA.ap(), 2.0, tmp8a.ap(),
                OP.mult, OP.add),
                wait=[LU.get(("AZ", q))], inc=None)
            vsp()
        v(lambda: vec.tensor_copy(khC.ap(), kh3[:, :, TO - 1:TO].squeeze(2)),
          inc=None)
        pro_last = None
        for j in range(NCH):
            pro_last = v(lambda j=j: vec.tensor_tensor(
                ab3[:, j, 0:TO - 1], kh3[:, j, 0:TO - 1], kh3[:, j, 1:TO],
                OP.subtract),
                wait=[LU.get(("AAB", q))])
        LU["AKH"] = pro_last
        v(lambda: vec.tensor_tensor(b1s.ap(), spB.ap(), tmp8b.ap(), OP.add),
          inc=None)
        vsp()
        n_it = len(inter)
        stride = max(1, (TO - 24) // n_it) if n_it else TO
        sp_old, sp_new = spB, spA
        chain_tok = None
        for tl in range(TO):
            zsl = zb3[:, :, tl:tl + 1].squeeze(2)
            v(lambda zsl=zsl: vec.tensor_single_scalar(dcs.ap(), zsl, 40.0, OP.min),
              inc="sCV")
            cv_val = S.val["sCV"]
            s(lambda: sca.activation(lps.ap(), dcs.ap(), AF.Exp),
              wait=[("sCV", cv_val)], inc=None)
            s(lambda: sca.activation(lps.ap(), lps.ap(), AF.Ln, bias=1.0), inc="sCS")
            cs_val = S.val["sCS"]
            if tl < TO - 1:
                v(lambda: vec.tensor_single_scalar(nus.ap(), b1s.ap(), 0.0, OP.max),
                  inc=None)
            inc_flag = "sV" if tl == TO - 1 else None
            tspn = v(lambda sp_new=sp_new, zsl=zsl: vec.tensor_tensor(
                sp_new.ap(), lps.ap(), zsl, OP.max),
                wait=[("sCS", cs_val)], inc=inc_flag)
            if tl < TO - 1:
                v(lambda tl=tl: vec.tensor_tensor(
                    nus.ap(), ab3[:, :, tl:tl + 1].squeeze(2), nus.ap(),
                    OP.subtract), inc=None)
                v(lambda sp_new=sp_new, tl=tl: vec.tensor_tensor(
                    b1s.ap(), sp_new.ap(), ab3[:, :, tl:tl + 1].squeeze(2),
                    OP.add), inc=None)
                v(lambda sp_new=sp_new, tl=tl: vec.scalar_tensor_tensor(
                    zb3[:, :, tl + 1:tl + 2].squeeze(2), sp_new.ap(), 2.0,
                    nus.ap(), OP.mult, OP.add), inc=None)
                vsp()
                if inter and tl % stride == 8 % stride:
                    inter.pop(0)()
            else:
                chain_tok = tspn
                vsp()
                v(lambda sp_old=sp_old: vec.tensor_tensor(
                    b1s.ap(), sp_old.ap(), ab3[:, :, TO - 2:TO - 1].squeeze(2),
                    OP.add), inc=None)
                vsp()
                v(lambda: vec.tensor_single_scalar(mzC.ap(), b1s.ap(), 0.0,
                                                   OP.max), inc=None)
            sp_old, sp_new = sp_new, sp_old
        if sp_old is not spA:
            chain_tok = v(lambda sp_old=sp_old: vec.tensor_copy(spA.ap(),
                                                                sp_old.ap()))
        for fn in inter:
            fn()
        state["first"] = False
        LU[("AAB", q)] = chain_tok
        TD["chain_tok"] = chain_tok

    def D_items(i, TD):
        p = i % 2
        zb3 = zfP[p]
        lu3 = abP[p]
        items = []
        TD["x1_sp"] = []

        def mk_j(j):
            def go():
                chain_tok = TD["chain_tok"]
                half = j % 2
                tv_ = ld(vch_h[half], s_v[i, :, j, :],
                         wait=TD["vr_toks"] + [LU.get(("vch", half)),
                                               LU.get("AVC")])
                tp_ = ld(pch_h[half], s_rp[i, :, j, :],
                         wait=TD["vr_toks"] + [LU.get(("pch", half)),
                                               LU.get("APC")])
                tx_ = ld(qch_h[half], s_xn[i, :, j, :],
                         wait=TD["xn_toks"] + [LU.get(("qch", half)),
                                               LU.get("bnc1")])
                t1 = v(lambda j=j: vec.tensor_tensor(
                    lu3[:, j, 0:TO], zb3[:, j, 0:TO],
                    ucol.ap()[:, j:j + 1].broadcast_to([128, TO]), OP.subtract),
                    wait=[chain_tok, LU.get(("AAB", p))])
                t2, _ = sigc(lu3[:, j, 0:TO], lu3[:, j, 0:TO], ASQ.ap()[:, 0:TO],
                             wait_first=[t1, LU.get("ASQ"), LU.get(("sqh", 0))])
                t3, _ = sigc(zb3[:, j, 0:TO], zb3[:, j, 0:TO], ASQ.ap()[:, 0:TO],
                             wait_first=[t1, t2])
                t4 = v(lambda j=j: vec.tensor_scalar(
                    bnc_h[0][:, 0:TO], zb3[:, j, 0:TO], -1.0, 1.0, OP.mult, OP.add),
                    wait=[t3, LU.get("bnc0")])
                v(lambda j=j: vec.tensor_copy(rb3[:, j, 0:1],
                                              rcar.ap()[:, j:j + 1]),
                  wait=[LU.get("AKK")], inc=None)
                t5 = v(lambda half=half: vec.tensor_tensor(
                    bnc_h[0][:, 0:TO], bnc_h[0][:, 0:TO], vch_h[half], OP.mult),
                    wait=[t4, tv_], inc=None)
                vsp()
                t7 = v(lambda j=j: vec.tensor_tensor_scan(
                    rb3[:, j, 1:TO + 1], zb3[:, j, 0:TO], bnc_h[0][:, 0:TO],
                    rcar.ap()[:, j:j + 1], OP.mult, OP.add), wait=[t5])
                LU["bnc0"] = t7
                vsp()
                v(lambda j=j: vec.tensor_copy(rcar.ap()[:, j:j + 1],
                                              rb3[:, j, TO:TO + 1]), inc=None)
                t9 = v(lambda j=j, half=half: vec.tensor_tensor(
                    bnc_h[0][:, 0:TO], rb3[:, j, 0:TO], vch_h[half], OP.subtract),
                    wait=[t7, tv_])
                LU["bnc0"] = t9
                vsp()
                v(lambda j=j: vec.tensor_tensor(
                    lu3[:, j, 0:TO], lu3[:, j, 0:TO], bnc_h[0][:, 0:TO], OP.mult),
                    wait=[t9, t2], inc=None)
                vsp()
                t11 = v(lambda j=j, half=half: vec.tensor_tensor(
                    lu3[:, j, 0:TO], lu3[:, j, 0:TO], vch_h[half], OP.add),
                    inc=None)
                t12, _ = sigc(pch_h[half], pch_h[half], ASQ.ap()[:, 0:TO],
                              wait_first=[tp_, t3])
                v(lambda j=j, half=half: vec.tensor_tensor(
                    pch_h[half], pch_h[half], lu3[:, j, 0:TO], OP.mult),
                    wait=[t12, t11], inc=None)
                vsp()
                t14 = v(lambda j=j, half=half: vec.tensor_tensor(
                    pch_h[half], pch_h[half], qch_h[half], OP.add), wait=[tx_])
                tk_sp = st(s_x1[i, :, j, :], pch_h[half], wait=[t14])
                TD["x1_sp"].append(tk_sp)
                LU[("pch", half)] = tk_sp
                LU[("qch", half)] = t14
                LU["bnc1"] = t14
                LU[("vch", half)] = t14
                LU["ASQ"] = t12
                LU[("sqh", 0)] = t12
                if j == NCH - 1:
                    LU[("AZ", p)] = tk_sp
                    LU[("AAB", p)] = tk_sp
                    LU["AKK"] = tk_sp
            return go
        for j in range(NCH):
            items.append(mk_j(j))
        return items

    def E_items(i, TD):
        p = i % 2
        lu3 = abP[p]
        h2f = zfP[p]
        items = []
        TD["q1_sp"] = []
        TD["kk_toks"] = {}

        def it_xstats():
            x1_toks = []
            for j in range(NCH):
                x1_toks.append(ld(lu3[:, j, 0:TO], s_x1[i, :, j, :],
                                  wait=TD["x1_sp"] + [LU.get(("AAB", p))]))
            TD["x1_toks"] = x1_toks
            mm = None
            for j in range(NCH):
                mm = t(lambda j=j: ten.matmul(
                    psS.ap()[0:1, 0:512], onesct.ap(), lu3[:, j, 0:TO],
                    start=(j == 0), stop=(j == NCH - 1)),
                    wait=[x1_toks[j], tok_oc, LU.get("psS")])
            ev = v(lambda: vec.tensor_copy(srow.ap()[0:1, 0:TO],
                                           psS.ap()[0:1, 0:512]), wait=[mm])
            LU["psS"] = ev
            TD["_ev1"] = ev
        items.append(it_xstats)

        def it_sqstats():
            x1_toks = TD["x1_toks"]
            mm2 = None
            for j in range(NCH):
                tsq = s(lambda j=j: sca.activation(
                    ASQ.ap()[:, 512 * (j % 2):512 * (j % 2) + 512],
                    lu3[:, j, 0:TO], AF.Square),
                    wait=[x1_toks[j], LU.get("ASQ"), LU.get(("sqh", j % 2))])
                mm2 = t(lambda j=j: ten.matmul(
                    psS.ap()[0:1, 0:512], onesct.ap(),
                    ASQ.ap()[:, 512 * (j % 2):512 * (j % 2) + 512],
                    start=(j == 0), stop=(j == NCH - 1)),
                    wait=[tsq, LU.get("psS")])
                LU[("sqh", j % 2)] = mm2
            ev2 = v(lambda: vec.tensor_copy(srow.ap()[0:1, TO:2 * TO],
                                            psS.ap()[0:1, 0:512]), wait=[mm2])
            LU["psS"] = ev2
            LU["ASQ"] = mm2
            ev = TD["_ev1"]
            v(lambda: vec.tensor_scalar_mul(srow.ap()[0:1, 0:TO],
                                            srow.ap()[0:1, 0:TO], 1.0 / C),
              wait=[ev, ev2], inc=None)
            v(lambda: vec.tensor_scalar_mul(srow.ap()[0:1, TO:2 * TO],
                                            srow.ap()[0:1, TO:2 * TO], 1.0 / C),
              inc=None)
            v(lambda: vec.tensor_tensor(srow.ap()[0:1, 2 * TO:3 * TO],
                                        srow.ap()[0:1, 0:TO],
                                        srow.ap()[0:1, 0:TO], OP.mult), inc=None)
            vsp()
            v(lambda: vec.tensor_tensor(srow.ap()[0:1, 2 * TO:3 * TO],
                                        srow.ap()[0:1, TO:2 * TO],
                                        srow.ap()[0:1, 2 * TO:3 * TO],
                                        OP.subtract), inc=None)
            vsp()
            r4 = v(lambda: vec.tensor_single_scalar(srow.ap()[0:1, 2 * TO:3 * TO],
                                                    srow.ap()[0:1, 2 * TO:3 * TO],
                                                    LN_EPS, OP.add))
            r5 = s(lambda: sca.activation(srow.ap()[0:1, 2 * TO:3 * TO],
                                          srow.ap()[0:1, 2 * TO:3 * TO], AF.Ln),
                   wait=[r4])
            r6 = s(lambda: sca.activation(srow.ap()[0:1, 2 * TO:3 * TO],
                                          srow.ap()[0:1, 2 * TO:3 * TO], AF.Exp,
                                          scale=-0.5), wait=[r5])
            bc_last = None
            for (src_ofs, dstb) in ((0, bcm), (2 * TO, bcr)):
                mmb = t(lambda src_ofs=src_ofs: ten.matmul(
                    psS.ap()[:, 0:512], onesrt.ap(),
                    srow.ap()[0:1, src_ofs:src_ofs + TO],
                    start=True, stop=True),
                    wait=[r6, tok_or, LU.get("psS")])
                bc_last = v(lambda dstb=dstb: vec.tensor_copy(
                    dstb.ap(), psS.ap()[:, 0:512]),
                    wait=[mmb, LU.get("bcmr")])
                LU["psS"] = bc_last
            TD["bc_last"] = bc_last
        items.append(it_sqstats)

        def it_h2():
            bc_last = TD["bc_last"]
            x1_toks = TD["x1_toks"]
            for j in range(NCH):
                v(lambda j=j: vec.tensor_tensor(
                    h2f[:, j, 1:TO + 1], lu3[:, j, 0:TO], bcm.ap(), OP.subtract),
                    wait=[bc_last, x1_toks[j], LU.get(("AZ", p))], inc=None)
                vsp()
                v(lambda j=j: vec.tensor_tensor(
                    h2f[:, j, 1:TO + 1], h2f[:, j, 1:TO + 1], bcr.ap(), OP.mult),
                    inc=None)
                vsp()
                v(lambda j=j: vec.tensor_scalar(
                    h2f[:, j, 1:TO + 1], h2f[:, j, 1:TO + 1],
                    g2col.ap()[:, j:j + 1], b2col.ap()[:, j:j + 1],
                    OP.mult, OP.add), inc=None)
            v(lambda: vec.tensor_copy(h2f[:, :, 0:1].squeeze(2), h2car.ap()),
              wait=[LU.get("h2car_w")], inc=None)
            h2_last = v(lambda: vec.tensor_copy(h2car.ap(),
                                                h2f[:, :, TO:TO + 1].squeeze(2)))
            LU["h2car_w"] = h2_last
            LU["bcmr"] = h2_last
            TD["h2_last"] = h2_last
        items.append(it_h2)

        def mk_ck(g):
            def go():
                h2_last = TD["h2_last"]
                kk_toks = []
                for hkl in range(4):
                    hk = 4 * g + hkl
                    wtk = ld(wck3,
                             Wck.rearrange("(dh p) h -> p dh h", p=128)
                             [:, :, 128 * hk:128 * hk + 128],
                             wait=[LU.get("AVC"), LU.get(("vch", 0)),
                                   LU.get(("vch", 1))])
                    ps = psA if hk % 2 == 0 else psB
                    bank = "psA" if hk % 2 == 0 else "psB"
                    mm = None
                    for ki in range(NCH):
                        mm = t(lambda ki=ki, ps=ps: ten.matmul(
                            ps.ap()[:, 0:TO], wck3[:, ki, :],
                            h2f[:, ki, 0:TO],
                            start=(ki == 0), stop=(ki == NCH - 1)),
                            wait=[wtk, LU.get(bank), h2_last])
                    LU["AVC"] = mm
                    LU[("vch", 0)] = mm
                    LU[("vch", 1)] = mm
                    sh = hkl % 2
                    e1 = s(lambda ps=ps, sh=sh: sca.activation(
                        ASQ.ap()[:, 512 * sh:512 * sh + 512], ps.ap()[:, 0:TO],
                        AF.Relu),
                        wait=[mm, LU.get(("sqh", sh)), LU.get("ASQ")])
                    e2 = s(lambda hkl=hkl, sh=sh: sca.activation(
                        kkg3[:, hkl, :], ASQ.ap()[:, 512 * sh:512 * sh + 512],
                        AF.Square),
                        wait=abn_wait())
                    LU[("sqh", sh)] = e2
                    LU[bank] = e1
                    kk_toks.append(e2)
                TD["kk_toks"][g] = kk_toks
            return go

        def mk_kv(g):
            def go():
                kk_toks = TD["kk_toks"][g]
                wtk2 = ld(wcv4,
                          Wcv.rearrange("(g4 hkl p) c -> g4 p hkl c", p=128,
                                        hkl=4)[g],
                          wait=[LU.get("AKK"), LU.get("AKKst")])
                g_mm_last = None
                for mi in range(NCH):
                    ps = psT if mi % 2 == 0 else psU
                    bank = "psT" if mi % 2 == 0 else "psU"
                    mm = None
                    for hkl in range(4):
                        mm = t(lambda hkl=hkl, mi=mi, ps=ps: ten.matmul(
                            ps.ap()[:, 0:TO],
                            wcv4[:, hkl, 128 * mi:128 * mi + 128],
                            kkg3[:, hkl, :],
                            start=(hkl == 0), stop=(hkl == 3)),
                            wait=[wtk2, kk_toks[hkl], LU.get(bank)])
                    g_mm_last = mm
                    if g == 0:
                        ev = v(lambda mi=mi, ps=ps: vec.tensor_copy(
                            kv3[:, mi, 0:TO], ps.ap()[:, 0:TO]),
                            wait=[mm, LU.get("AKKkv")])
                    else:
                        ev = v(lambda mi=mi, ps=ps: vec.tensor_tensor(
                            kv3[:, mi, 0:TO], kv3[:, mi, 0:TO], ps.ap()[:, 0:TO],
                            OP.add), wait=[mm])
                    LU[bank] = ev
                    if g == 7:
                        TD.setdefault("kv_evs", []).append(ev)
                abn_set(g_mm_last)
                LU["AKKst"] = g_mm_last
                if g == 7:
                    LU["AKK"] = g_mm_last
            return go
        for g in range(8):
            items.append(mk_ck(g))
            items.append(mk_kv(g))

        def it_cr_load():
            w0 = ld(wS3[0], Wcr.rearrange("(dh p) c -> p dh c", p=128)
                    [:, :, 0:512], wait=[LU.get(("AWS", 0))])
            w1_ = ld(wS3[1], Wcr.rearrange("(dh p) c -> p dh c", p=128)
                     [:, :, 512:1024], wait=[LU.get(("AWS", 1))])
            TD["cr_wtok"] = [w0, w1_]
        items.append(it_cr_load)

        def mk_cr(mi):
            def go():
                sl = mi // 4
                wtok = TD["cr_wtok"][sl]
                ps = psA if mi % 2 == 0 else psB
                bank = "psA" if mi % 2 == 0 else "psB"
                mm = None
                for ki in range(NCH):
                    mm = t(lambda ki=ki, mi=mi, ps=ps, sl=sl: ten.matmul(
                        ps.ap(), wS3[sl][:, ki, 128 * (mi - 4 * sl):
                                         128 * (mi - 4 * sl) + 128],
                        h2f[:, ki, 0:TO],
                        start=(ki == 0), stop=(ki == NCH - 1)),
                        wait=[wtok, LU.get(bank), TD["h2_last"]])
                LU[("AWS", sl)] = mm
                e1, e1r = sigc(bnc_h[0][:, 0:TO], ps.ap()[:, 0:TO],
                               ASQ.ap()[:, 0:TO],
                               wait_first=[mm, LU.get("ASQ"),
                                           LU.get(("sqh", 0))],
                               wait_last=[LU.get("bnc0")])
                LU["ASQ"] = e1
                LU[("sqh", 0)] = e1
                LU[bank] = e1r
                v(lambda mi=mi: vec.tensor_tensor(
                    bnc_h[0][:, 0:TO], bnc_h[0][:, 0:TO], kv3[:, mi, 0:TO],
                    OP.mult),
                    wait=[e1] + TD.get("kv_evs", []), inc=None)
                vsp()
                e3 = v(lambda mi=mi: vec.tensor_tensor(
                    bnc_h[0][:, 0:TO], bnc_h[0][:, 0:TO], lu3[:, mi, 0:TO],
                    OP.add),
                    wait=[TD["x1_toks"][mi]])
                tk = st(s_q1[i, :, mi, :], bnc_h[0][:, 0:TO], wait=[e3])
                TD["q1_sp"].append(tk)
                LU["bnc0"] = tk
                if mi == NCH - 1:
                    LU[("AAB", p)] = tk
                    LU[("AZ", p)] = mm
                    LU["AKKkv"] = tk
            return go
        for mi in range(NCH):
            items.append(mk_cr(mi))
        return items

    def F1_items(i, TD):
        items = []
        st_ = {}

        def emit_xn_loads():
            toks = []
            for j in range(NCH):
                toks.append(ld(xnf3[:, j, :], s_xn[i, :, j, :],
                               wait=TD["xn_toks"] + [LU.get("AH")]))
            st_["xn"] = toks
        items.append(emit_xn_loads)

        def mk_hq(hk):
            def go():
                wtk = ld(AVC.ap().rearrange("p (dh q) -> p dh q", q=128),
                         w1T.rearrange("(dh p) h -> p dh h", p=128)
                         [:, :, 128 * hk:128 * hk + 128],
                         wait=[LU.get("AVC"), LU.get(("vch", 0)),
                               LU.get(("vch", 1))] + pp_toks)
                ps = psA if hk % 2 == 0 else psB
                bank = "psA" if hk % 2 == 0 else "psB"
                mm = None
                for ki in range(NCH):
                    mm = t(lambda ki=ki, ps=ps: ten.matmul(
                        ps.ap(),
                        AVC.ap().rearrange("p (dh q) -> p dh q", q=128)[:, ki, :],
                        xnf3[:, ki, :],
                        start=(ki == 0), stop=(ki == NCH - 1)),
                        wait=[wtk, st_["xn"][ki], LU.get(bank)])
                LU["AVC"] = mm
                LU[("vch", 0)] = mm
                LU[("vch", 1)] = mm
                ev = s(lambda hk=hk, ps=ps: sca.activation(
                    hq3[:, hk, :], ps.ap(), AF.Relu, bias=fb1.ap()[:, hk:hk + 1]),
                    wait=[mm, tok_b1, LU.get("AKK")])
                LU[bank] = ev
                st_.setdefault("hq", []).append(ev)
                if hk == NHH - 1:
                    LU["AH"] = mm
            return go
        for hk in range(NHH):
            items.append(mk_hq(hk))

        def mk_fq(mi):
            def go():
                wtk = ld(pch8.rearrange("p (hh q) -> p hh q", q=128),
                         wqT.rearrange("(hh p) c -> p hh c", p=128)
                         [:, :, 128 * mi:128 * mi + 128],
                         wait=[LU.get("APC"), LU.get(("pch", 0)),
                               LU.get(("pch", 1))] + pp_toks)
                ps = psA if mi % 2 == 0 else psB
                bank = "psA" if mi % 2 == 0 else "psB"
                mm = None
                for hk in range(NHH):
                    mm = t(lambda hk=hk, ps=ps: ten.matmul(
                        ps.ap(),
                        pch8.rearrange("p (hh q) -> p hh q", q=128)[:, hk, :],
                        hq3[:, hk, :],
                        start=(hk == 0), stop=(hk == NHH - 1)),
                        wait=[wtk, st_["hq"][hk], LU.get(bank)])
                LU["APC"] = mm
                LU[("pch", 0)] = mm
                LU[("pch", 1)] = mm
                ev = v(lambda mi=mi, ps=ps: vec.tensor_single_scalar(
                    fq3[:, mi, :], ps.ap(), bqcol.ap()[:, mi:mi + 1], OP.add),
                    wait=[mm, tok_bq, LU.get("FQF")])
                LU[bank] = ev
                TD.setdefault("fq_toks", []).append(ev)
                if mi == NCH - 1:
                    LU["AKK"] = mm
            return go
        for mi in range(NCH):
            items.append(mk_fq(mi))
        return items

    def F2_items(i, TD):
        items = []

        def mk_ch(ch):
            def go():
                g = i * TOC + ch
                ta = ld(xch_h[0], acc3[g],
                        wait=TD["acc_toks"] + [LU.get(("xch", 0)),
                                               LU.get(("xch", 1))])
                evs = []
                for j in range(NCH):
                    tq = ld(ABN.ap()[:, C + 128 * j:C + 128 * j + 128],
                            s_q1[i, :, j, 128 * ch:128 * ch + 128],
                            wait=TD["q1_sp"] + [LU.get(("qch", 0)),
                                                LU.get(("qch", 1)),
                                                LU.get("bnc1")])
                    t1 = v(lambda j=j, ch=ch: vec.tensor_tensor(
                        stg.ap(), ABN.ap()[:, C + 128 * j:C + 128 * j + 128],
                        fq3[:, j, 128 * ch:128 * ch + 128], OP.add),
                        wait=[tq] + TD["fq_toks"] + [LU.get("stg")])
                    pb1 = psT if j % 2 == 0 else psV
                    pk1 = "psT" if j % 2 == 0 else "psV"
                    tk = t(lambda pb1=pb1: ten.transpose(pb1.ap()[:, 0:128],
                                                         stg.ap(), identt.ap()),
                           wait=[t1, tok_id, LU.get(pk1)])
                    LU["stg"] = tk
                    ev = v(lambda j=j, pb1=pb1: vec.tensor_tensor(
                        bnc_h[0][:, 128 * j:128 * j + 128], pb1.ap()[:, 0:128],
                        xch_h[0][:, 128 * j:128 * j + 128], OP.add),
                        wait=[tk, ta, LU.get("bnc0")])
                    LU[pk1] = ev
                    evs.append(ev)
                out_tok = st(y3[g], bnc_h[0], wait=evs)
                LU["bnc0"] = out_tok
                LU[("xch", 0)] = out_tok
                LU[("xch", 1)] = out_tok
                LU[("qch", 0)] = out_tok
                LU[("qch", 1)] = out_tok
                LU["bnc1"] = out_tok
                if ch == TOC - 1:
                    LU["FQF"] = out_tok
                state["final_tok"] = out_tok
            return go
        for ch in range(TOC):
            items.append(mk_ch(ch))
        return items

    TD = {i: {} for i in range(n_tiles)}
    for fn in A_items(0, TD[0]) + B_items(0, TD[0]):
        fn()
    for i in range(n_tiles):
        items = []
        if i >= 1:
            items += F1_items(i - 1, TD[i - 1])
            items += D_items(i - 1, TD[i - 1])
            items += E_items(i - 1, TD[i - 1])
            items += F2_items(i - 1, TD[i - 1])
        if i + 1 < n_tiles:
            items += A_items(i + 1, TD[i + 1])
            items += B_items(i + 1, TD[i + 1])
        stage_C(i, TD[i], items)
    il = n_tiles - 1
    for fn in (F1_items(il, TD[il]) + D_items(il, TD[il])
               + E_items(il, TD[il]) + F2_items(il, TD[il])):
        fn()

    import contextlib
    stack = contextlib.ExitStack()
    sems = {}
    sem_names = (["sT", "sV", "sS", "sCV", "sCS"]
                 + [f"sLD{i}" for i in range(NLANES_LD)]
                 + [f"sST{i}" for i in range(NLANES_ST)])
    for name in sem_names:
        sems[name] = stack.enter_context(nc.semaphore(name))
    with nc.Block() as blk:
        @blk.sync
        def _(eng):
            S.replay(SY, sems, eng)

        @blk.tensor
        def _(eng):
            S.replay(T_, sems, eng)

        @blk.vector
        def _(eng):
            S.replay(V_, sems, eng)

        @blk.scalar
        def _(eng):
            S.replay(S_, sems, eng)

        @blk.gpsimd
        def _(eng):
            S.replay(GP, sems, eng)
            for i in range(NLANES_ST):
                nm = f"sST{i}"
                if S.val.get(nm, 0):
                    eng.wait_ge(sems[nm], S.val[nm])
    stack.close()
    return nc


_NC_CACHE = {}

def _get_program(n_tiles=16):
    if n_tiles not in _NC_CACHE:
        _NC_CACHE[n_tiles] = build_program(n_tiles)
    return _NC_CACHE[n_tiles]


def _fm_vec(vv):
    return np.ascontiguousarray(np.asarray(vv, np.float32).reshape(-1, 128).T)


def make_common(Wk, Wv, Wr, Wck, Wcv, Wcr, w1, w2, tdec, tfirst, tshiftv,
                g1, b1, g2, b2, fb1, fb2):
    return {
        "Wk": np.ascontiguousarray(Wk, np.float32),
        "Wv": np.ascontiguousarray(Wv, np.float32),
        "Wr": np.ascontiguousarray(Wr, np.float32),
        "Wck": np.ascontiguousarray(Wck, np.float32),
        "Wcv": np.ascontiguousarray(Wcv, np.float32),
        "Wcr": np.ascontiguousarray(Wcr, np.float32),
        "w1": np.ascontiguousarray(w1, np.float32),
        "w2": np.ascontiguousarray(w2, np.float32),
        "tdec": _fm_vec(tdec), "tfirst": _fm_vec(tfirst),
        "tshift": _fm_vec(np.asarray(tshiftv, np.float32).reshape(-1)),
        "ln1g": _fm_vec(g1), "ln1b": _fm_vec(b1),
        "ln2g": _fm_vec(g2), "ln2b": _fm_vec(b2),
        "b1v": _fm_vec(fb1), "b2v": _fm_vec(fb2),
        "ident": np.eye(128, dtype=np.float32),
        "onesc": np.ones((128, 1), np.float32),
        "onesr": np.ones((1, 128), np.float32),
        "nramp": np.ascontiguousarray(
            np.broadcast_to(-(np.arange(T, dtype=np.float32) + 1.0), (128, T))),
    }


def kernel(L0_attn_x, L0_attn_time_decay, L0_attn_time_first, L0_attn_time_shift,
           L0_attn_Wk, L0_attn_Wv, L0_attn_Wr, L0_attn_Wck, L0_attn_Wcv, L0_attn_Wcr,
           L0_attn_ln1_g, L0_attn_ln1_b, L0_attn_ln2_g, L0_attn_ln2_b,
           L0_ffn_w1, L0_ffn_b1, L0_ffn_w2, L0_ffn_b2):
    x = np.ascontiguousarray(np.asarray(L0_attn_x, dtype=np.float32))
    nc = _get_program(16)
    common = make_common(L0_attn_Wk, L0_attn_Wv, L0_attn_Wr, L0_attn_Wck, L0_attn_Wcv,
                         L0_attn_Wcr, L0_ffn_w1, L0_ffn_w2, L0_attn_time_decay,
                         L0_attn_time_first, L0_attn_time_shift, L0_attn_ln1_g,
                         L0_attn_ln1_b, L0_attn_ln2_g, L0_attn_ln2_b, L0_ffn_b1,
                         L0_ffn_b2)
    in_maps = [dict(common, x=np.ascontiguousarray(x[b])) for b in range(B)]
    res = run_bass_kernel_spmd(nc, in_maps, core_ids=list(range(B)))
    return np.stack([np.asarray(res.results[b]["y"]) for b in range(B)], axis=0)


# version 06
# speedup vs baseline: 1.2506x; 1.2506x over previous
import numpy as np
import ml_dtypes
import concourse.bass as bass
import concourse.mybir as mybir
from concourse.bass_utils import run_bass_kernel_spmd

F32 = mybir.dt.float32
FP8 = mybir.dt.float8e4
FP16 = mybir.dt.float16
OP = mybir.AluOpType
AF = mybir.ActivationFunctionType

B, T, C, H = 8, 8192, 1024, 4096
NCH = C // 128
NHH = H // 128
LN_EPS = 1e-5
NEG_BIG = -1e38
TO = 512
TOC = TO // 128
V_, S_, T_, SY, GP = "vector", "scalar", "tensor", "sync", "gpsimd"


class Sched:
    def __init__(self):
        self.items = []
        self.val = {}

    def emit(self, eng, fn, wait=(), inc=None, inc_n=1):
        w = [t for t in wait if t is not None]
        tok = None
        if inc is not None:
            self.val[inc] = self.val.get(inc, 0) + inc_n
            tok = (inc, self.val[inc])
        self.items.append((eng, fn, w, (inc, inc_n) if inc else None))
        return tok

    def replay(self, engine_name, sems, eng):
        last_wait = {}
        for (e, fn, waits, inc) in self.items:
            if e != engine_name:
                continue
            for (sem, val) in waits:
                if last_wait.get(sem, -1) >= val:
                    continue
                last_wait[sem] = val
                eng.wait_ge(sems[sem], val)
            ins = fn()
            if inc is not None and ins is not None:
                ins.then_inc(sems[inc[0]], inc[1])


def build_program(n_tiles=16, dbg=False):
    nc = bass.Bass("TRN2", target_bir_lowering=False,
                   detect_race_conditions=False)

    def ein(name, shape, dt=F32):
        return nc.dram_tensor(name, shape, dt, kind="ExternalInput").ap()

    x_in = ein("x", [T, C])
    Wk = ein("Wk", [C, C]); Wv = ein("Wv", [C, C]); Wr = ein("Wr", [C, C])
    Wck = ein("Wck", [C, H]); Wcv = ein("Wcv", [H, C]); Wcr = ein("Wcr", [C, C])
    w1 = ein("w1", [H, C]); w2 = ein("w2", [C, H])
    tdec = ein("tdec", [128, NCH]); tfirst = ein("tfirst", [128, NCH])
    tshift = ein("tshift", [128, NCH])
    ln1g = ein("ln1g", [128, NCH]); ln1b = ein("ln1b", [128, NCH])
    ln2g = ein("ln2g", [128, NCH]); ln2b = ein("ln2b", [128, NCH])
    b1v = ein("b1v", [128, NHH]); b2v = ein("b2v", [128, NCH])
    ident = ein("ident", [128, 128])
    onesc = ein("onesc", [128, 1])
    onesr = ein("onesr", [1, 128])
    nramp = ein("nramp", [128, T])

    y_out = nc.dram_tensor("y", [T, C], F32, kind="ExternalOutput").ap()

    def scr(name, shape, dt=F32):
        kind = "ExternalOutput" if dbg else "Internal"
        return nc.dram_tensor(name, shape, dt, kind=kind).ap()

    w1T = scr("w1T", [C, H])
    wqT = scr("wqT", [H, C], FP8)
    s_xn = scr("s_xn", [n_tiles, 128, NCH, TO])
    s_acc = scr("s_acc", [n_tiles * TO, C])
    s_v = scr("s_v", [n_tiles, 128, NCH, TO])
    s_rp = scr("s_rp", [n_tiles, 128, NCH, TO])
    s_x1 = scr("s_x1", [n_tiles, 128, NCH, TO])
    s_q1 = scr("s_q1", [n_tiles, 128, NCH, TO])

    sb_used = [0]
    def sb(name, cols, dt=F32):
        t = nc.alloc_sbuf_tensor(name, [128, cols], dt)
        sb_used[0] += cols * mybir.dt.size(dt)
        return t

    wcol = sb("wcol", NCH); ucol = sb("ucol", NCH); tscol = sb("tscol", NCH)
    g1col = sb("g1col", NCH); b1col = sb("b1col", NCH)
    g2col = sb("g2col", NCH); b2col = sb("b2col", NCH)
    fb1 = sb("fb1", NHH); bqcol = sb("bqcol", NCH); b2ft = sb("b2ft", NCH)
    identt = sb("identt", 128)
    onesct = nc.alloc_sbuf_tensor("onesct", [128, 1], F32)
    onesrt = nc.alloc_sbuf_tensor("onesrt", [1, 128], F32)
    sb_used[0] += 2 * 128 * 4
    spA = sb("spA", NCH); spB = sb("spB", NCH)
    mzC = sb("mzC", NCH); khC = sb("khC", NCH)
    rcar = sb("rcar", NCH); hcar = sb("hcar", NCH); h2car = sb("h2car", NCH)
    dcs = sb("dcs", NCH); lps = sb("lps", NCH)
    b1s = sb("b1s", NCH); nus = sb("nus", NCH)
    tmp8a = sb("tmp8a", NCH); tmp8b = sb("tmp8b", NCH)
    junk = sb("junk", 8)
    stat = sb("stat", 64)
    stg = sb("stg", 128)
    nrp = sb("nrp", TO)
    bcm = sb("bcm", TO); bcr = sb("bcr", TO)
    srow = nc.alloc_sbuf_tensor("srow", [1, 3 * TO], F32)
    sb_used[0] += 3 * TO * 4

    AH = sb("AH", NCH * (TO + 1))
    AKH = sb("AKH", NCH * TO)
    AAB = sb("AAB", NCH * TO)
    AABB = sb("AABB", NCH * TO)
    AZ = sb("AZ", NCH * (TO + 1))
    AZB = sb("AZB", NCH * (TO + 1))
    AKK = sb("AKK", 2 * NCH * TO)
    AWA = sb("AWA", NCH * C)
    AX = sb("AX", C)
    ASQ = sb("ASQ", C)
    ABN = sb("ABN", 2 * C)
    AVC = sb("AVC", C)
    APC = sb("APC", C)
    FQF = sb("FQF", NCH * TO, FP16)
    assert sb_used[0] <= 221 * 1024, f"SBUF over: {sb_used[0]/1024:.1f} KiB/part"

    psA = nc.alloc_psum_tensor("psA", [128, 512], F32)
    psC = nc.alloc_psum_tensor("psC", [128, 512], F32)
    psB = nc.alloc_psum_tensor("psB", [128, 512], F32)
    psT = nc.alloc_psum_tensor("psT", [128, 512], F32)
    psU = nc.alloc_psum_tensor("psU", [128, 512], F32)
    psS = nc.alloc_psum_tensor("psS", [128, 512], F32)
    psV = nc.alloc_psum_tensor("psV", [128, 512], F32)
    psW = nc.alloc_psum_tensor("psW", [128, 512], F32)

    S = Sched()
    vec, sca, ten = nc.vector, nc.scalar, nc.tensor
    LU = {}

    NLANES_LD, NLANES_ST = 12, 6
    lane_state = {"ld": 0, "st": 0}

    def ld(dst_ap, src_ap, wait=()):
        lane = lane_state["ld"]; lane_state["ld"] = (lane + 1) % NLANES_LD
        sem = f"sLD{lane}"
        prev = S.val.get(sem, 0)
        return S.emit(SY, lambda d=dst_ap, s=src_ap: nc.sync.dma_start(out=d, in_=s),
                      wait=list(wait) + [(sem, prev)], inc=sem, inc_n=16)

    def st(dst_ap, src_ap, wait=()):
        lane = lane_state["st"]; lane_state["st"] = (lane + 1) % NLANES_ST
        sem = f"sST{lane}"
        prev = S.val.get(sem, 0)
        return S.emit(GP, lambda d=dst_ap, s=src_ap: nc.gpsimd.dma_start(out=d, in_=s),
                      wait=list(wait) + [(sem, prev)], inc=sem, inc_n=16)

    def v(fn, wait=(), inc="sV"):
        return S.emit(V_, fn, wait=wait, inc=inc)

    def s(fn, wait=(), inc="sS"):
        return S.emit(S_, fn, wait=wait, inc=inc)

    def t(fn, wait=(), inc="sT"):
        return S.emit(T_, fn, wait=wait, inc=inc)

    def vsp():
        S.emit(V_, lambda: vec.tensor_scalar_mul(junk.ap(), junk.ap(), 1.0),
               wait=(), inc=None)

    def sigc(dst_ap, src_ap, tmp_ap, wait_first=(), wait_last=()):
        c1 = v(lambda: vec.tensor_single_scalar(tmp_ap, src_ap, 40.0, OP.min),
               wait=list(wait_first))
        s(lambda: sca.activation(tmp_ap, tmp_ap, AF.Exp), wait=[c1], inc=None)
        c3 = s(lambda: sca.activation(tmp_ap, tmp_ap, AF.Ln, bias=1.0))
        v(lambda: vec.tensor_tensor(tmp_ap, tmp_ap, src_ap, OP.max),
          wait=[c3], inc=None)
        c5 = v(lambda: vec.tensor_tensor(tmp_ap, src_ap, tmp_ap, OP.subtract))
        c6 = s(lambda: sca.activation(dst_ap, tmp_ap, AF.Exp),
               wait=[c5] + list(wait_last))
        return c6, c5

    def abn_wait():
        return [LU.get("bnc0"), LU.get("bnc1"), LU.get(("qch", 0)),
                LU.get(("qch", 1))]

    def abn_set(tok):
        LU["bnc0"] = tok; LU["bnc1"] = tok
        LU[("qch", 0)] = tok; LU[("qch", 1)] = tok

    mmb_state = {"i": 0}

    def slc_wait(sp):
        if sp == 0:
            return [LU.get("AVC"), LU.get(("vch", 0)), LU.get(("vch", 1))]
        return [LU.get("APC"), LU.get(("pch", 0)), LU.get(("pch", 1))]

    def slc_set(sp, tok):
        if sp == 0:
            LU["AVC"] = tok; LU[("vch", 0)] = tok; LU[("vch", 1)] = tok
        else:
            LU["APC"] = tok; LU[("pch", 0)] = tok; LU[("pch", 1)] = tok

    def next_mmbank():
        b = ((psA, "psA"), (psB, "psB"), (psC, "psC"))[mmb_state["i"] % 3]
        mmb_state["i"] += 1
        return b

    hb3 = AH.ap().rearrange("p (c t) -> p c t", c=NCH)
    kh3 = AKH.ap().rearrange("p (c t) -> p c t", c=NCH)
    abP = [AAB.ap().rearrange("p (c t) -> p c t", c=NCH),
           AABB.ap().rearrange("p (c t) -> p c t", c=NCH)]
    zfP = [AZ.ap().rearrange("p (c t) -> p c t", c=NCH),
           AZB.ap().rearrange("p (c t) -> p c t", c=NCH)]
    hq3 = AKK.ap().bitcast(FP8)[:, 0:NHH * TO].rearrange("p (c t) -> p c t", c=NHH)
    rb3 = AKK.ap()[:, 0:NCH * (TO + 1)].rearrange("p (c t) -> p c t", c=NCH)
    kv3 = AKK.ap()[:, NCH * TO:2 * NCH * TO].rearrange("p (c t) -> p c t", c=NCH)
    wcv4 = AKK.ap()[:, 0:4 * C].rearrange("p (h c) -> p h c", h=4)
    kkg3 = ABN.ap().rearrange("p (h t) -> p h t", h=4)
    fq3 = FQF.ap().rearrange("p (c t) -> p c t", c=NCH)
    xnf3 = AH.ap()[:, 0:NCH * TO].rearrange("p (c t) -> p c t", c=NCH)
    wS3 = [AWA.ap()[:, 0:NCH * TO].rearrange("p (c t) -> p c t", c=NCH),
           AWA.ap()[:, NCH * TO:2 * NCH * TO].rearrange("p (c t) -> p c t", c=NCH)]
    wck3P = [AVC.ap().rearrange("p (dh q) -> p dh q", q=128),
             APC.ap().rearrange("p (dh q) -> p dh q", q=128)]
    wck3 = wck3P[0]
    vch_h = [AVC.ap()[:, 0:TO], AVC.ap()[:, TO:2 * TO]]
    pch_h = [APC.ap()[:, 0:TO], APC.ap()[:, TO:2 * TO]]
    qch_h = [ABN.ap()[:, C:C + TO], ABN.ap()[:, C + TO:2 * C]]
    bnc_h = [ABN.ap()[:, 0:C], ABN.ap()[:, C:2 * C]]
    xch_h = [AX.ap()[:, 0:C], AX.ap()[:, 0:C]]
    pch8 = APC.ap().bitcast(FP8)[:, 0:H]
    w8P = [pch8.rearrange("p (hh q) -> p hh q", q=128),
           AVC.ap().bitcast(FP8)[:, 0:H].rearrange("p (hh q) -> p hh q", q=128)]
    wcv4P = [wcv4,
             AH.ap()[:, 0:4 * C].rearrange("p (h c) -> p h c", h=4)]

    v(lambda: vec.memset(junk.ap(), 0.0), inc=None)
    tok_id = ld(identt.ap(), ident)
    tok_oc = ld(onesct.ap(), onesc)
    tok_or = ld(onesrt.ap(), onesr)
    cvl = {}
    for (nm, dst, src) in (("w", wcol, tdec), ("u", ucol, tfirst), ("ts", tscol, tshift),
                           ("g1", g1col, ln1g), ("b1", b1col, ln1b),
                           ("g2", g2col, ln2g), ("b2", b2col, ln2b)):
        cvl[nm] = ld(dst.ap(), src)
    tok_b1 = ld(fb1.ap(), b1v)
    tok_b2f = ld(b2ft.ap(), b2v)
    c1 = v(lambda: vec.tensor_copy(tmp8a.ap().bitcast(FP16)[:, 0:NCH], b2ft.ap()),
           wait=[tok_b2f])
    vsp()
    tok_bq = v(lambda: vec.tensor_copy(bqcol.ap(), tmp8a.ap().bitcast(FP16)[:, 0:NCH]),
               wait=[c1])

    pp_toks = []
    pp_tok = None
    for hb in range(NHH):
        wload = ld(AX.ap()[:, 0:C], w1.rearrange("(hb p) c -> hb p c", p=128)[hb],
                   wait=[LU.get("AX"), pp_tok])
        ev = None
        for dc in range(NCH):
            pb = psT if dc % 2 == 0 else psV
            pbk = "psT" if dc % 2 == 0 else "psV"
            tk = t(lambda dc=dc, pb=pb: ten.transpose(
                pb.ap()[:, 0:128],
                AX.ap()[:, 128 * dc:128 * dc + 128], identt.ap()),
                wait=[wload, tok_id, LU.get(pbk)])
            ev = v(lambda dc=dc, pb=pb: vec.tensor_copy(
                ABN.ap()[:, 128 * dc:128 * dc + 128], pb.ap()[:, 0:128]),
                wait=[tk, LU.get("bnc0")])
            LU[pbk] = ev
        LU["AX"] = ev
        pp_tok = st(w1T.rearrange("(cb p) h -> p cb h", p=128)[:, :, 128 * hb:128 * hb + 128],
                    ABN.ap()[:, 0:C].rearrange("p (cb q) -> p cb q", q=128),
                    wait=[ev])
        pp_toks.append(pp_tok)
        LU["bnc0"] = pp_tok
        LU[("xch", 0)] = ev
        LU[("xch", 1)] = ev
    for cb in range(NCH):
        wload = ld(AWA.ap()[:, 0:H], w2.rearrange("(cb p) h -> cb p h", p=128)[cb],
                   wait=[LU.get(("AWS", 0)), LU.get(("AWS", 1)), pp_tok])
        ev = None
        for hb in range(NHH):
            pb = psU if hb % 2 == 0 else psW
            pbk = "psU" if hb % 2 == 0 else "psW"
            tk = t(lambda hb=hb, pb=pb: ten.transpose(
                pb.ap()[:, 0:128],
                AWA.ap()[:, 128 * hb:128 * hb + 128], identt.ap()),
                wait=[wload, tok_id, LU.get(pbk)])
            ev = v(lambda hb=hb, pb=pb: vec.tensor_copy(
                ABN.ap().bitcast(FP8)[:, 128 * hb:128 * hb + 128],
                pb.ap()[:, 0:128]),
                wait=[tk, LU.get("bnc0")])
            LU[pbk] = ev
        LU[("AWS", 0)] = ev
        LU[("AWS", 1)] = ev
        pp_tok = st(wqT.rearrange("(hb p) c -> p hb c", p=128)[:, :, 128 * cb:128 * cb + 128],
                    ABN.ap().bitcast(FP8)[:, 0:H].rearrange("p (hb q) -> p hb q", q=128),
                    wait=[ev])
        pp_toks.append(pp_tok)
        LU["bnc0"] = pp_tok

    v(lambda: vec.memset(spA.ap(), NEG_BIG), wait=[tok_bq])
    v(lambda: vec.memset(spB.ap(), NEG_BIG))
    v(lambda: vec.memset(mzC.ap(), 0.0))
    v(lambda: vec.memset(khC.ap(), 0.0))
    v(lambda: vec.memset(tmp8b.ap(), 0.0))
    v(lambda: vec.memset(rcar.ap(), 0.0))
    v(lambda: vec.tensor_copy(hcar.ap(), tscol.ap()), wait=[cvl["ts"]])
    init_tok = v(lambda: vec.tensor_copy(h2car.ap(), tscol.ap()))
    init_all = [init_tok, tok_b1, tok_oc, tok_or, tok_id] + list(cvl.values())

    x3 = x_in.rearrange("(n p) c -> n p c", p=128)
    acc3 = s_acc.rearrange("(n p) c -> n p c", p=128)
    y3 = y_out.rearrange("(n p) c -> n p c", p=128)

    state = {"first": True, "final_tok": None}

    def A_items(i, TD):
        TD["xn_toks"] = []
        TD["acc_toks"] = []
        items = []

        def mk_ch(ch):
            def go():
                g = i * TOC + ch
                half = ch % 2
                xs = xch_h[half]
                tok_x = ld(xs, x3[g], wait=[LU.get(("xch", 0)), LU.get(("xch", 1))])
                c0 = 16 * ch
                st_ = stat.ap()
                v(lambda xs=xs, c0=c0, st_=st_: vec.reduce_sum(
                    st_[:, c0:c0 + 1], xs, axis=mybir.AxisListType.X),
                    wait=[tok_x], inc=None)
                t_sq = s(lambda xs=xs: sca.activation(ASQ.ap(), xs, AF.Square),
                         wait=[tok_x, LU.get("ASQ")])
                v(lambda c0=c0, st_=st_: vec.reduce_sum(
                    st_[:, c0 + 1:c0 + 2], ASQ.ap(), axis=mybir.AxisListType.X),
                    wait=[t_sq], inc=None)
                v(lambda c0=c0, st_=st_: vec.tensor_scalar_mul(
                    st_[:, c0:c0 + 1], st_[:, c0:c0 + 1], 1.0 / C), inc=None)
                v(lambda c0=c0, st_=st_: vec.tensor_scalar_mul(
                    st_[:, c0 + 1:c0 + 2], st_[:, c0 + 1:c0 + 2], 1.0 / C), inc=None)
                v(lambda c0=c0, st_=st_: vec.tensor_tensor(
                    st_[:, c0 + 2:c0 + 3], st_[:, c0:c0 + 1], st_[:, c0:c0 + 1],
                    OP.mult), inc=None)
                vsp()
                v(lambda c0=c0, st_=st_: vec.tensor_tensor(
                    st_[:, c0 + 2:c0 + 3], st_[:, c0 + 1:c0 + 2], st_[:, c0 + 2:c0 + 3],
                    OP.subtract), inc=None)
                vsp()
                t_var = v(lambda c0=c0, st_=st_: vec.tensor_single_scalar(
                    st_[:, c0 + 2:c0 + 3], st_[:, c0 + 2:c0 + 3], LN_EPS, OP.add))
                t_ln = s(lambda c0=c0, st_=st_: sca.activation(
                    st_[:, c0 + 2:c0 + 3], st_[:, c0 + 2:c0 + 3], AF.Ln),
                    wait=[t_var])
                t_rs = s(lambda c0=c0, st_=st_: sca.activation(
                    st_[:, c0 + 2:c0 + 3], st_[:, c0 + 2:c0 + 3], AF.Exp, scale=-0.5),
                    wait=[t_ln])
                t_xn = v(lambda xs=xs, c0=c0, st_=st_: vec.tensor_scalar(
                    ASQ.ap(), xs, st_[:, c0:c0 + 1], st_[:, c0 + 2:c0 + 3],
                    OP.subtract, OP.mult), wait=[t_rs])
                tk_acc = st(acc3[g], xs, wait=[t_xn, LU.get(("xch", half))])
                TD["acc_toks"].append(tk_acc)
                LU["bnc0"] = tk_acc
                v(lambda c0=c0, st_=st_: vec.reduce_sum(
                    st_[:, c0 + 3:c0 + 4], ASQ.ap(), axis=mybir.AxisListType.X),
                    wait=[t_xn], inc=None)
                t_sq2 = s(lambda: sca.activation(bnc_h[1], ASQ.ap(), AF.Square),
                          wait=[t_xn, LU.get("bnc1"), LU.get(("qch", 0)),
                                LU.get(("qch", 1))])
                t_sxx2 = v(lambda c0=c0, st_=st_: vec.reduce_sum(
                    st_[:, c0 + 4:c0 + 5], bnc_h[1], axis=mybir.AxisListType.X),
                    wait=[t_sq2])
                LU["bnc1"] = t_sxx2
                LU[("qch", 0)] = t_sxx2
                LU[("qch", 1)] = t_sxx2
                v(lambda c0=c0, st_=st_: vec.tensor_scalar_mul(
                    st_[:, c0 + 3:c0 + 4], st_[:, c0 + 3:c0 + 4], 1.0 / C), inc=None)
                v(lambda c0=c0, st_=st_: vec.tensor_scalar_mul(
                    st_[:, c0 + 4:c0 + 5], st_[:, c0 + 4:c0 + 5], 1.0 / C), inc=None)
                v(lambda c0=c0, st_=st_: vec.tensor_tensor(
                    st_[:, c0 + 5:c0 + 6], st_[:, c0 + 3:c0 + 4], st_[:, c0 + 3:c0 + 4],
                    OP.mult), inc=None)
                vsp()
                v(lambda c0=c0, st_=st_: vec.tensor_tensor(
                    st_[:, c0 + 5:c0 + 6], st_[:, c0 + 4:c0 + 5], st_[:, c0 + 5:c0 + 6],
                    OP.subtract), inc=None)
                vsp()
                t_var2 = v(lambda c0=c0, st_=st_: vec.tensor_single_scalar(
                    st_[:, c0 + 5:c0 + 6], st_[:, c0 + 5:c0 + 6], LN_EPS, OP.add))
                t_ln2 = s(lambda c0=c0, st_=st_: sca.activation(
                    st_[:, c0 + 5:c0 + 6], st_[:, c0 + 5:c0 + 6], AF.Ln),
                    wait=[t_var2])
                t_rs2 = s(lambda c0=c0, st_=st_: sca.activation(
                    st_[:, c0 + 5:c0 + 6], st_[:, c0 + 5:c0 + 6], AF.Exp,
                    scale=-0.5), wait=[t_ln2])
                t_h = v(lambda xs=xs, c0=c0, st_=st_: vec.tensor_scalar(
                    xs, ASQ.ap(), st_[:, c0 + 3:c0 + 4], st_[:, c0 + 5:c0 + 6],
                    OP.subtract, OP.mult), wait=[t_rs2, tk_acc])
                ev_xn = None
                tk1_last = None
                hlast = None
                for j in range(NCH):
                    pb1 = psT if j % 2 == 0 else psV
                    pk1 = "psT" if j % 2 == 0 else "psV"
                    pb2 = psU if j % 2 == 0 else psW
                    pk2 = "psU" if j % 2 == 0 else "psW"
                    tk1 = t(lambda j=j, pb1=pb1: ten.transpose(
                        pb1.ap()[:, 0:128],
                        ASQ.ap()[:, 128 * j:128 * j + 128], identt.ap()),
                        wait=[t_xn, tok_id, LU.get(pk1)])
                    tk1_last = tk1
                    ev_xn = v(lambda j=j, pb1=pb1: vec.tensor_copy(
                        ABN.ap()[:, 128 * j:128 * j + 128],
                        pb1.ap()[:, 0:128]),
                        wait=[tk1, LU.get("bnc0")])
                    LU[pk1] = ev_xn
                    tk2 = t(lambda j=j, pb2=pb2, xs=xs: ten.transpose(
                        pb2.ap()[:, 0:128],
                        xs[:, 128 * j:128 * j + 128], identt.ap()),
                        wait=[t_h, tok_id, LU.get(pk2)])
                    hlast = v(lambda j=j, pb2=pb2, ch=ch: vec.tensor_scalar(
                        hb3[:, j, 1 + 128 * ch:1 + 128 * ch + 128],
                        pb2.ap()[:, 0:128],
                        g1col.ap()[:, j:j + 1], b1col.ap()[:, j:j + 1],
                        OP.mult, OP.add),
                        wait=[tk2, LU.get("AH")] + init_all)
                    LU[pk2] = hlast
                LU[("xch", half)] = hlast
                tk_xn = st(s_xn[i, :, :, 128 * ch:128 * ch + 128],
                           ABN.ap()[:, 0:C].rearrange("p (c q) -> p c q", c=NCH),
                           wait=[ev_xn])
                TD["xn_toks"].append(tk_xn)
                LU["bnc0"] = tk_xn
                LU["ASQ"] = tk1_last
                TD["_hlast"] = hlast
            return go
        for ch in range(TOC):
            items.append(mk_ch(ch))

        def it_lead():
            t_lead = v(lambda: vec.tensor_copy(hb3[:, :, 0:1].squeeze(2), hcar.ap()),
                       wait=[LU.get("AH")] + init_all)
            t_car = v(lambda: vec.tensor_copy(hcar.ap(), hb3[:, :, TO:TO + 1].squeeze(2)),
                      wait=[t_lead])
            TD["hready"] = [TD["_hlast"], t_lead, t_car]
        items.append(it_lead)
        return items

    def B_items(i, TD):
        TD["vr_toks"] = []
        items = []
        tbase = i * TO

        def mk_wh(idx, kind, Wsrc, half):
            def go():
                if kind == "k" and half == 0:
                    TD["tok_nr"] = ld(nrp.ap(), nramp[:, tbase:tbase + TO],
                                      wait=[LU.get("nrp")])
                sl = (2 * idx + half) % 2
                wtok = ld(wS3[sl], Wsrc.rearrange("(dh p) c -> p dh c", p=128)
                          [:, :, 512 * half:512 * half + 512],
                          wait=[LU.get(("AWS", sl))])
                mm_last = None
                for mi in range(4 * half, 4 * half + 4):
                    ps, bank = next_mmbank()
                    mm = None
                    for ki in range(NCH):
                        mm = t(lambda ki=ki, mi=mi, ps=ps, sl=sl, half=half:
                               ten.matmul(
                                   ps.ap(),
                                   wS3[sl][:, ki, 128 * (mi - 4 * half):
                                           128 * (mi - 4 * half) + 128],
                                   hb3[:, ki, 0:TO],
                                   start=(ki == 0), stop=(ki == NCH - 1)),
                               wait=[wtok, LU.get(bank)]
                               + (TD["hready"] if ki == 0 else []))
                    mm_last = mm
                    if kind == "k":
                        ev = v(lambda mi=mi, ps=ps: vec.scalar_tensor_tensor(
                            kh3[:, mi, 0:TO], nrp.ap(),
                            wcol.ap()[:, mi:mi + 1], ps.ap(), OP.mult, OP.add),
                            wait=[mm, TD["tok_nr"], LU.get("AKH")])
                        TD["kh_ready"] = ev
                    else:
                        bh = mi % 2
                        ev = v(lambda ps=ps, bh=bh: vec.tensor_copy(
                            bnc_h[bh][:, 0:TO], ps.ap()),
                            wait=[mm, LU.get("bnc" + str(bh))])
                        dst = s_v if kind == "v" else s_rp
                        tk2 = st(dst[i, :, mi, :], bnc_h[bh][:, 0:TO], wait=[ev])
                        TD["vr_toks"].append(tk2)
                        LU["bnc" + str(bh)] = tk2
                    LU[bank] = ev
                LU[("AWS", sl)] = mm_last
                LU["AH"] = mm_last
                if kind == "k" and half == 1:
                    LU["nrp"] = TD["kh_ready"]
            return go
        for idx, (kind, Wsrc) in enumerate([("k", Wk), ("v", Wv), ("r", Wr)]):
            for half in range(2):
                items.append(mk_wh(idx, kind, Wsrc, half))
        return items

    def stage_C(i, TD, interleave=()):
        q = i % 2
        zb3 = zfP[q]
        ab3 = abP[q]
        kh_ready = TD["kh_ready"]
        inter = list(interleave)
        if state["first"]:
            v(lambda: vec.memset(zb3[:, :, 0:1].squeeze(2), NEG_BIG),
              wait=[kh_ready, LU.get(("AZ", q))], inc=None)
            vsp()
        else:
            v(lambda: vec.tensor_tensor(
                tmp8b.ap(), khC.ap(), kh3[:, :, 0:1].squeeze(2), OP.subtract),
                wait=[kh_ready], inc=None)
            vsp()
            v(lambda: vec.tensor_tensor(tmp8a.ap(), tmp8b.ap(), mzC.ap(),
                                        OP.subtract), inc=None)
            vsp()
            v(lambda: vec.scalar_tensor_tensor(
                zb3[:, :, 0:1].squeeze(2), spA.ap(), 2.0, tmp8a.ap(),
                OP.mult, OP.add),
                wait=[LU.get(("AZ", q))], inc=None)
            vsp()
        v(lambda: vec.tensor_copy(khC.ap(), kh3[:, :, TO - 1:TO].squeeze(2)),
          inc=None)
        pro_last = None
        for j in range(NCH):
            pro_last = v(lambda j=j: vec.tensor_tensor(
                ab3[:, j, 0:TO - 1], kh3[:, j, 0:TO - 1], kh3[:, j, 1:TO],
                OP.subtract),
                wait=[LU.get(("AAB", q))])
        LU["AKH"] = pro_last
        v(lambda: vec.tensor_tensor(b1s.ap(), spB.ap(), tmp8b.ap(), OP.add),
          inc=None)
        vsp()
        n_it = len(inter)
        stride = max(1, (TO - 24) // n_it) if n_it else TO
        sp_old, sp_new = spB, spA
        chain_tok = None
        for tl in range(TO):
            zsl = zb3[:, :, tl:tl + 1].squeeze(2)
            v(lambda zsl=zsl: vec.tensor_single_scalar(dcs.ap(), zsl, 40.0, OP.min),
              inc="sCV")
            cv_val = S.val["sCV"]
            s(lambda: sca.activation(lps.ap(), dcs.ap(), AF.Exp),
              wait=[("sCV", cv_val)], inc=None)
            s(lambda: sca.activation(lps.ap(), lps.ap(), AF.Ln, bias=1.0), inc="sCS")
            cs_val = S.val["sCS"]
            if tl < TO - 1:
                v(lambda: vec.tensor_single_scalar(nus.ap(), b1s.ap(), 0.0, OP.max),
                  inc=None)
            inc_flag = "sV" if tl == TO - 1 else None
            tspn = v(lambda sp_new=sp_new, zsl=zsl: vec.tensor_tensor(
                sp_new.ap(), lps.ap(), zsl, OP.max),
                wait=[("sCS", cs_val)], inc=inc_flag)
            if tl < TO - 1:
                v(lambda tl=tl: vec.tensor_tensor(
                    nus.ap(), ab3[:, :, tl:tl + 1].squeeze(2), nus.ap(),
                    OP.subtract), inc=None)
                v(lambda sp_new=sp_new, tl=tl: vec.tensor_tensor(
                    b1s.ap(), sp_new.ap(), ab3[:, :, tl:tl + 1].squeeze(2),
                    OP.add), inc=None)
                v(lambda sp_new=sp_new, tl=tl: vec.scalar_tensor_tensor(
                    zb3[:, :, tl + 1:tl + 2].squeeze(2), sp_new.ap(), 2.0,
                    nus.ap(), OP.mult, OP.add), inc=None)
                vsp()
                if inter and tl % stride == 8 % stride:
                    inter.pop(0)()
            else:
                chain_tok = tspn
                vsp()
                v(lambda sp_old=sp_old: vec.tensor_tensor(
                    b1s.ap(), sp_old.ap(), ab3[:, :, TO - 2:TO - 1].squeeze(2),
                    OP.add), inc=None)
                vsp()
                v(lambda: vec.tensor_single_scalar(mzC.ap(), b1s.ap(), 0.0,
                                                   OP.max), inc=None)
            sp_old, sp_new = sp_new, sp_old
        if sp_old is not spA:
            chain_tok = v(lambda sp_old=sp_old: vec.tensor_copy(spA.ap(),
                                                                sp_old.ap()))
        for fn in inter:
            fn()
        state["first"] = False
        LU[("AAB", q)] = chain_tok
        TD["chain_tok"] = chain_tok

    def D_items(i, TD):
        p = i % 2
        zb3 = zfP[p]
        lu3 = abP[p]
        items = []
        TD["x1_sp"] = []

        def mk_j(j):
            def go():
                chain_tok = TD["chain_tok"]
                half = j % 2
                tv_ = ld(vch_h[half], s_v[i, :, j, :],
                         wait=TD["vr_toks"] + [LU.get(("vch", half)),
                                               LU.get("AVC")])
                tp_ = ld(pch_h[half], s_rp[i, :, j, :],
                         wait=TD["vr_toks"] + [LU.get(("pch", half)),
                                               LU.get("APC")])
                tx_ = ld(qch_h[half], s_xn[i, :, j, :],
                         wait=TD["xn_toks"] + [LU.get(("qch", half)),
                                               LU.get("bnc1")])
                t1 = v(lambda j=j: vec.tensor_tensor(
                    lu3[:, j, 0:TO], zb3[:, j, 0:TO],
                    ucol.ap()[:, j:j + 1].broadcast_to([128, TO]), OP.subtract),
                    wait=[chain_tok, LU.get(("AAB", p))])
                t2, _ = sigc(lu3[:, j, 0:TO], lu3[:, j, 0:TO], ASQ.ap()[:, 0:TO],
                             wait_first=[t1, LU.get("ASQ"), LU.get(("sqh", 0))])
                t3, _ = sigc(zb3[:, j, 0:TO], zb3[:, j, 0:TO], ASQ.ap()[:, 0:TO],
                             wait_first=[t1, t2])
                t4 = v(lambda j=j: vec.tensor_scalar(
                    bnc_h[0][:, 0:TO], zb3[:, j, 0:TO], -1.0, 1.0, OP.mult, OP.add),
                    wait=[t3, LU.get("bnc0")])
                v(lambda j=j: vec.tensor_copy(rb3[:, j, 0:1],
                                              rcar.ap()[:, j:j + 1]),
                  wait=[LU.get("AKK")], inc=None)
                t5 = v(lambda half=half: vec.tensor_tensor(
                    bnc_h[0][:, 0:TO], bnc_h[0][:, 0:TO], vch_h[half], OP.mult),
                    wait=[t4, tv_], inc=None)
                vsp()
                t7 = v(lambda j=j: vec.tensor_tensor_scan(
                    rb3[:, j, 1:TO + 1], zb3[:, j, 0:TO], bnc_h[0][:, 0:TO],
                    rcar.ap()[:, j:j + 1], OP.mult, OP.add), wait=[t5])
                LU["bnc0"] = t7
                vsp()
                v(lambda j=j: vec.tensor_copy(rcar.ap()[:, j:j + 1],
                                              rb3[:, j, TO:TO + 1]), inc=None)
                t9 = v(lambda j=j, half=half: vec.tensor_tensor(
                    bnc_h[0][:, 0:TO], rb3[:, j, 0:TO], vch_h[half], OP.subtract),
                    wait=[t7, tv_])
                LU["bnc0"] = t9
                vsp()
                v(lambda j=j: vec.tensor_tensor(
                    lu3[:, j, 0:TO], lu3[:, j, 0:TO], bnc_h[0][:, 0:TO], OP.mult),
                    wait=[t9, t2], inc=None)
                vsp()
                t11 = v(lambda j=j, half=half: vec.tensor_tensor(
                    lu3[:, j, 0:TO], lu3[:, j, 0:TO], vch_h[half], OP.add),
                    inc=None)
                t12, _ = sigc(pch_h[half], pch_h[half], ASQ.ap()[:, 0:TO],
                              wait_first=[tp_, t3])
                v(lambda j=j, half=half: vec.tensor_tensor(
                    pch_h[half], pch_h[half], lu3[:, j, 0:TO], OP.mult),
                    wait=[t12, t11], inc=None)
                vsp()
                t14 = v(lambda j=j, half=half: vec.tensor_tensor(
                    pch_h[half], pch_h[half], qch_h[half], OP.add), wait=[tx_])
                tk_sp = st(s_x1[i, :, j, :], pch_h[half], wait=[t14])
                TD["x1_sp"].append(tk_sp)
                LU[("pch", half)] = tk_sp
                LU[("qch", half)] = t14
                LU["bnc1"] = t14
                LU[("vch", half)] = t14
                LU["ASQ"] = t12
                LU[("sqh", 0)] = t12
                if j == NCH - 1:
                    LU[("AZ", p)] = tk_sp
                    LU[("AAB", p)] = tk_sp
                    LU["AKK"] = tk_sp
            return go
        for j in range(NCH):
            items.append(mk_j(j))
        return items

    def E_items(i, TD):
        p = i % 2
        lu3 = abP[p]
        h2f = zfP[p]
        items = []
        TD["q1_sp"] = []
        TD["kk_toks"] = {}

        def it_xstats():
            x1_toks = []
            for j in range(NCH):
                x1_toks.append(ld(lu3[:, j, 0:TO], s_x1[i, :, j, :],
                                  wait=TD["x1_sp"] + [LU.get(("AAB", p))]))
            TD["x1_toks"] = x1_toks
            mm = None
            for j in range(NCH):
                mm = t(lambda j=j: ten.matmul(
                    psS.ap()[0:1, 0:512], onesct.ap(), lu3[:, j, 0:TO],
                    start=(j == 0), stop=(j == NCH - 1)),
                    wait=[x1_toks[j], tok_oc, LU.get("psS")])
            ev = v(lambda: vec.tensor_copy(srow.ap()[0:1, 0:TO],
                                           psS.ap()[0:1, 0:512]), wait=[mm])
            LU["psS"] = ev
            TD["_ev1"] = ev
        items.append(it_xstats)

        def it_sqstats():
            x1_toks = TD["x1_toks"]
            mm2 = None
            for j in range(NCH):
                tsq = s(lambda j=j: sca.activation(
                    ASQ.ap()[:, 512 * (j % 2):512 * (j % 2) + 512],
                    lu3[:, j, 0:TO], AF.Square),
                    wait=[x1_toks[j], LU.get("ASQ"), LU.get(("sqh", j % 2))])
                mm2 = t(lambda j=j: ten.matmul(
                    psS.ap()[0:1, 0:512], onesct.ap(),
                    ASQ.ap()[:, 512 * (j % 2):512 * (j % 2) + 512],
                    start=(j == 0), stop=(j == NCH - 1)),
                    wait=[tsq, LU.get("psS")])
                LU[("sqh", j % 2)] = mm2
            ev2 = v(lambda: vec.tensor_copy(srow.ap()[0:1, TO:2 * TO],
                                            psS.ap()[0:1, 0:512]), wait=[mm2])
            LU["psS"] = ev2
            LU["ASQ"] = mm2
            ev = TD["_ev1"]
            v(lambda: vec.tensor_scalar_mul(srow.ap()[0:1, 0:TO],
                                            srow.ap()[0:1, 0:TO], 1.0 / C),
              wait=[ev, ev2], inc=None)
            v(lambda: vec.tensor_scalar_mul(srow.ap()[0:1, TO:2 * TO],
                                            srow.ap()[0:1, TO:2 * TO], 1.0 / C),
              inc=None)
            v(lambda: vec.tensor_tensor(srow.ap()[0:1, 2 * TO:3 * TO],
                                        srow.ap()[0:1, 0:TO],
                                        srow.ap()[0:1, 0:TO], OP.mult), inc=None)
            vsp()
            v(lambda: vec.tensor_tensor(srow.ap()[0:1, 2 * TO:3 * TO],
                                        srow.ap()[0:1, TO:2 * TO],
                                        srow.ap()[0:1, 2 * TO:3 * TO],
                                        OP.subtract), inc=None)
            vsp()
            r4 = v(lambda: vec.tensor_single_scalar(srow.ap()[0:1, 2 * TO:3 * TO],
                                                    srow.ap()[0:1, 2 * TO:3 * TO],
                                                    LN_EPS, OP.add))
            r5 = s(lambda: sca.activation(srow.ap()[0:1, 2 * TO:3 * TO],
                                          srow.ap()[0:1, 2 * TO:3 * TO], AF.Ln),
                   wait=[r4])
            r6 = s(lambda: sca.activation(srow.ap()[0:1, 2 * TO:3 * TO],
                                          srow.ap()[0:1, 2 * TO:3 * TO], AF.Exp,
                                          scale=-0.5), wait=[r5])
            bc_last = None
            for (src_ofs, dstb) in ((0, bcm), (2 * TO, bcr)):
                mmb = t(lambda src_ofs=src_ofs: ten.matmul(
                    psS.ap()[:, 0:512], onesrt.ap(),
                    srow.ap()[0:1, src_ofs:src_ofs + TO],
                    start=True, stop=True),
                    wait=[r6, tok_or, LU.get("psS")])
                bc_last = v(lambda dstb=dstb: vec.tensor_copy(
                    dstb.ap(), psS.ap()[:, 0:512]),
                    wait=[mmb, LU.get("bcmr")])
                LU["psS"] = bc_last
            TD["bc_last"] = bc_last
        items.append(it_sqstats)

        def it_h2():
            bc_last = TD["bc_last"]
            x1_toks = TD["x1_toks"]
            for j in range(NCH):
                v(lambda j=j: vec.tensor_tensor(
                    h2f[:, j, 1:TO + 1], lu3[:, j, 0:TO], bcm.ap(), OP.subtract),
                    wait=[bc_last, x1_toks[j], LU.get(("AZ", p))], inc=None)
                vsp()
                v(lambda j=j: vec.tensor_tensor(
                    h2f[:, j, 1:TO + 1], h2f[:, j, 1:TO + 1], bcr.ap(), OP.mult),
                    inc=None)
                vsp()
                v(lambda j=j: vec.tensor_scalar(
                    h2f[:, j, 1:TO + 1], h2f[:, j, 1:TO + 1],
                    g2col.ap()[:, j:j + 1], b2col.ap()[:, j:j + 1],
                    OP.mult, OP.add), inc=None)
            v(lambda: vec.tensor_copy(h2f[:, :, 0:1].squeeze(2), h2car.ap()),
              wait=[LU.get("h2car_w")], inc=None)
            h2_last = v(lambda: vec.tensor_copy(h2car.ap(),
                                                h2f[:, :, TO:TO + 1].squeeze(2)))
            LU["h2car_w"] = h2_last
            LU["bcmr"] = h2_last
            TD["h2_last"] = h2_last
        items.append(it_h2)

        def mk_ck(g):
            def go():
                h2_last = TD["h2_last"]
                kk_toks = []
                for hkl in range(4):
                    hk = 4 * g + hkl
                    sp = hk % 2
                    wtk = ld(wck3P[sp],
                             Wck.rearrange("(dh p) h -> p dh h", p=128)
                             [:, :, 128 * hk:128 * hk + 128],
                             wait=slc_wait(sp))
                    ps, bank = next_mmbank()
                    mm = None
                    for ki in range(NCH):
                        mm = t(lambda ki=ki, ps=ps, sp=sp: ten.matmul(
                            ps.ap()[:, 0:TO], wck3P[sp][:, ki, :],
                            h2f[:, ki, 0:TO],
                            start=(ki == 0), stop=(ki == NCH - 1)),
                            wait=[wtk, LU.get(bank), h2_last])
                    slc_set(sp, mm)
                    sh = hkl % 2
                    e1 = s(lambda ps=ps, sh=sh: sca.activation(
                        ASQ.ap()[:, 512 * sh:512 * sh + 512], ps.ap()[:, 0:TO],
                        AF.Relu),
                        wait=[mm, LU.get(("sqh", sh)), LU.get("ASQ")])
                    e2 = s(lambda hkl=hkl, sh=sh: sca.activation(
                        kkg3[:, hkl, :], ASQ.ap()[:, 512 * sh:512 * sh + 512],
                        AF.Square),
                        wait=abn_wait())
                    LU[("sqh", sh)] = e2
                    LU[bank] = e1
                    kk_toks.append(e2)
                TD["kk_toks"][g] = kk_toks
            return go

        def mk_kv(g):
            def go():
                kk_toks = TD["kk_toks"][g]
                wp = g % 2
                wtk2 = ld(wcv4P[wp],
                          Wcv.rearrange("(g4 hkl p) c -> g4 p hkl c", p=128,
                                        hkl=4)[g],
                          wait=([LU.get("AKK"), LU.get("AKKst")] if wp == 0
                                else [LU.get("AH"), LU.get("AHst")]))
                g_mm_last = None
                for mi in range(NCH):
                    ps = psT if mi % 2 == 0 else psU
                    bank = "psT" if mi % 2 == 0 else "psU"
                    mm = None
                    for hkl in range(4):
                        mm = t(lambda hkl=hkl, mi=mi, ps=ps, wp=wp: ten.matmul(
                            ps.ap()[:, 0:TO],
                            wcv4P[wp][:, hkl, 128 * mi:128 * mi + 128],
                            kkg3[:, hkl, :],
                            start=(hkl == 0), stop=(hkl == 3)),
                            wait=[wtk2, kk_toks[hkl], LU.get(bank)])
                    g_mm_last = mm
                    if g == 0:
                        ev = v(lambda mi=mi, ps=ps: vec.tensor_copy(
                            kv3[:, mi, 0:TO], ps.ap()[:, 0:TO]),
                            wait=[mm, LU.get("AKKkv")])
                    else:
                        ev = v(lambda mi=mi, ps=ps: vec.tensor_tensor(
                            kv3[:, mi, 0:TO], kv3[:, mi, 0:TO], ps.ap()[:, 0:TO],
                            OP.add), wait=[mm])
                    LU[bank] = ev
                    if g == 7:
                        TD.setdefault("kv_evs", []).append(ev)
                abn_set(g_mm_last)
                if wp == 0:
                    LU["AKKst"] = g_mm_last
                    if g == 6:
                        LU["AKK"] = g_mm_last
                else:
                    LU["AHst"] = g_mm_last
                    LU["AH"] = g_mm_last
            return go
        for g in range(8):
            items.append(mk_ck(g))
            items.append(mk_kv(g))

        def it_cr_load():
            w0 = ld(wS3[0], Wcr.rearrange("(dh p) c -> p dh c", p=128)
                    [:, :, 0:512], wait=[LU.get(("AWS", 0))])
            w1_ = ld(wS3[1], Wcr.rearrange("(dh p) c -> p dh c", p=128)
                     [:, :, 512:1024], wait=[LU.get(("AWS", 1))])
            TD["cr_wtok"] = [w0, w1_]
        items.append(it_cr_load)

        def mk_cr(mi):
            def go():
                sl = mi // 4
                wtok = TD["cr_wtok"][sl]
                ps, bank = next_mmbank()
                mm = None
                for ki in range(NCH):
                    mm = t(lambda ki=ki, mi=mi, ps=ps, sl=sl: ten.matmul(
                        ps.ap(), wS3[sl][:, ki, 128 * (mi - 4 * sl):
                                         128 * (mi - 4 * sl) + 128],
                        h2f[:, ki, 0:TO],
                        start=(ki == 0), stop=(ki == NCH - 1)),
                        wait=[wtok, LU.get(bank), TD["h2_last"]])
                LU[("AWS", sl)] = mm
                e1, e1r = sigc(bnc_h[0][:, 0:TO], ps.ap()[:, 0:TO],
                               ASQ.ap()[:, 0:TO],
                               wait_first=[mm, LU.get("ASQ"),
                                           LU.get(("sqh", 0))],
                               wait_last=[LU.get("bnc0")])
                LU["ASQ"] = e1
                LU[("sqh", 0)] = e1
                LU[bank] = e1r
                v(lambda mi=mi: vec.tensor_tensor(
                    bnc_h[0][:, 0:TO], bnc_h[0][:, 0:TO], kv3[:, mi, 0:TO],
                    OP.mult),
                    wait=[e1] + TD.get("kv_evs", []), inc=None)
                vsp()
                e3 = v(lambda mi=mi: vec.tensor_tensor(
                    bnc_h[0][:, 0:TO], bnc_h[0][:, 0:TO], lu3[:, mi, 0:TO],
                    OP.add),
                    wait=[TD["x1_toks"][mi]])
                tk = st(s_q1[i, :, mi, :], bnc_h[0][:, 0:TO], wait=[e3])
                TD["q1_sp"].append(tk)
                LU["bnc0"] = tk
                if mi == NCH - 1:
                    LU[("AAB", p)] = tk
                    LU[("AZ", p)] = mm
                    LU["AKKkv"] = tk
            return go
        for mi in range(NCH):
            items.append(mk_cr(mi))
        return items

    def F1_items(i, TD):
        items = []
        st_ = {}

        def emit_xn_loads():
            toks = []
            for j in range(NCH):
                toks.append(ld(xnf3[:, j, :], s_xn[i, :, j, :],
                               wait=TD["xn_toks"] + [LU.get("AH")]))
            st_["xn"] = toks
        items.append(emit_xn_loads)

        def mk_hq(hk):
            def go():
                sp = hk % 2
                wtk = ld(wck3P[sp],
                         w1T.rearrange("(dh p) h -> p dh h", p=128)
                         [:, :, 128 * hk:128 * hk + 128],
                         wait=slc_wait(sp) + pp_toks)
                ps, bank = next_mmbank()
                mm = None
                for ki in range(NCH):
                    mm = t(lambda ki=ki, ps=ps, sp=sp: ten.matmul(
                        ps.ap(),
                        wck3P[sp][:, ki, :],
                        xnf3[:, ki, :],
                        start=(ki == 0), stop=(ki == NCH - 1)),
                        wait=[wtk, st_["xn"][ki], LU.get(bank)])
                slc_set(sp, mm)
                ev = s(lambda hk=hk, ps=ps: sca.activation(
                    hq3[:, hk, :], ps.ap(), AF.Relu, bias=fb1.ap()[:, hk:hk + 1]),
                    wait=[mm, tok_b1, LU.get("AKK")])
                LU[bank] = ev
                st_.setdefault("hq", []).append(ev)
                if hk == NHH - 1:
                    LU["AH"] = mm
            return go
        for hk in range(NHH):
            items.append(mk_hq(hk))

        def mk_fq(mi):
            def go():
                sp = mi % 2
                wtk = ld(w8P[sp],
                         wqT.rearrange("(hh p) c -> p hh c", p=128)
                         [:, :, 128 * mi:128 * mi + 128],
                         wait=slc_wait(1 - sp) + pp_toks)
                ps, bank = next_mmbank()
                mm = None
                for hk in range(NHH):
                    mm = t(lambda hk=hk, ps=ps, sp=sp: ten.matmul(
                        ps.ap(),
                        w8P[sp][:, hk, :],
                        hq3[:, hk, :],
                        start=(hk == 0), stop=(hk == NHH - 1)),
                        wait=[wtk, st_["hq"][hk], LU.get(bank)])
                slc_set(1 - sp, mm)
                ev = v(lambda mi=mi, ps=ps: vec.tensor_single_scalar(
                    fq3[:, mi, :], ps.ap(), bqcol.ap()[:, mi:mi + 1], OP.add),
                    wait=[mm, tok_bq, LU.get("FQF")])
                LU[bank] = ev
                TD.setdefault("fq_toks", []).append(ev)
                if mi == NCH - 1:
                    LU["AKK"] = mm
            return go
        for mi in range(NCH):
            items.append(mk_fq(mi))
        return items

    def F2_items(i, TD):
        items = []

        def mk_ch(ch):
            def go():
                g = i * TOC + ch
                ta = ld(xch_h[0], acc3[g],
                        wait=TD["acc_toks"] + [LU.get(("xch", 0)),
                                               LU.get(("xch", 1))])
                evs = []
                for j in range(NCH):
                    tq = ld(ABN.ap()[:, C + 128 * j:C + 128 * j + 128],
                            s_q1[i, :, j, 128 * ch:128 * ch + 128],
                            wait=TD["q1_sp"] + [LU.get(("qch", 0)),
                                                LU.get(("qch", 1)),
                                                LU.get("bnc1")])
                    t1 = v(lambda j=j, ch=ch: vec.tensor_tensor(
                        stg.ap(), ABN.ap()[:, C + 128 * j:C + 128 * j + 128],
                        fq3[:, j, 128 * ch:128 * ch + 128], OP.add),
                        wait=[tq] + TD["fq_toks"] + [LU.get("stg")])
                    pb1 = psT if j % 2 == 0 else psV
                    pk1 = "psT" if j % 2 == 0 else "psV"
                    tk = t(lambda pb1=pb1: ten.transpose(pb1.ap()[:, 0:128],
                                                         stg.ap(), identt.ap()),
                           wait=[t1, tok_id, LU.get(pk1)])
                    LU["stg"] = tk
                    ev = v(lambda j=j, pb1=pb1: vec.tensor_tensor(
                        bnc_h[0][:, 128 * j:128 * j + 128], pb1.ap()[:, 0:128],
                        xch_h[0][:, 128 * j:128 * j + 128], OP.add),
                        wait=[tk, ta, LU.get("bnc0")])
                    LU[pk1] = ev
                    evs.append(ev)
                out_tok = st(y3[g], bnc_h[0], wait=evs)
                LU["bnc0"] = out_tok
                LU[("xch", 0)] = out_tok
                LU[("xch", 1)] = out_tok
                LU[("qch", 0)] = out_tok
                LU[("qch", 1)] = out_tok
                LU["bnc1"] = out_tok
                if ch == TOC - 1:
                    LU["FQF"] = out_tok
                state["final_tok"] = out_tok
            return go
        for ch in range(TOC):
            items.append(mk_ch(ch))
        return items

    TD = {i: {} for i in range(n_tiles)}
    for fn in A_items(0, TD[0]) + B_items(0, TD[0]):
        fn()
    for i in range(n_tiles):
        items = []
        if i >= 1:
            items += F1_items(i - 1, TD[i - 1])
            items += D_items(i - 1, TD[i - 1])
            items += E_items(i - 1, TD[i - 1])
            items += F2_items(i - 1, TD[i - 1])
        if i + 1 < n_tiles:
            items += A_items(i + 1, TD[i + 1])
            items += B_items(i + 1, TD[i + 1])
        stage_C(i, TD[i], items)
    il = n_tiles - 1
    for fn in (F1_items(il, TD[il]) + D_items(il, TD[il])
               + E_items(il, TD[il]) + F2_items(il, TD[il])):
        fn()

    import contextlib
    stack = contextlib.ExitStack()
    sems = {}
    sem_names = (["sT", "sV", "sS", "sCV", "sCS"]
                 + [f"sLD{i}" for i in range(NLANES_LD)]
                 + [f"sST{i}" for i in range(NLANES_ST)])
    for name in sem_names:
        sems[name] = stack.enter_context(nc.semaphore(name))
    with nc.Block() as blk:
        @blk.sync
        def _(eng):
            S.replay(SY, sems, eng)

        @blk.tensor
        def _(eng):
            S.replay(T_, sems, eng)

        @blk.vector
        def _(eng):
            S.replay(V_, sems, eng)

        @blk.scalar
        def _(eng):
            S.replay(S_, sems, eng)

        @blk.gpsimd
        def _(eng):
            S.replay(GP, sems, eng)
            for i in range(NLANES_ST):
                nm = f"sST{i}"
                if S.val.get(nm, 0):
                    eng.wait_ge(sems[nm], S.val[nm])
    stack.close()
    return nc


_NC_CACHE = {}

def _get_program(n_tiles=16):
    if n_tiles not in _NC_CACHE:
        _NC_CACHE[n_tiles] = build_program(n_tiles)
    return _NC_CACHE[n_tiles]


def _fm_vec(vv):
    return np.ascontiguousarray(np.asarray(vv, np.float32).reshape(-1, 128).T)


def make_common(Wk, Wv, Wr, Wck, Wcv, Wcr, w1, w2, tdec, tfirst, tshiftv,
                g1, b1, g2, b2, fb1, fb2):
    return {
        "Wk": np.ascontiguousarray(Wk, np.float32),
        "Wv": np.ascontiguousarray(Wv, np.float32),
        "Wr": np.ascontiguousarray(Wr, np.float32),
        "Wck": np.ascontiguousarray(Wck, np.float32),
        "Wcv": np.ascontiguousarray(Wcv, np.float32),
        "Wcr": np.ascontiguousarray(Wcr, np.float32),
        "w1": np.ascontiguousarray(w1, np.float32),
        "w2": np.ascontiguousarray(w2, np.float32),
        "tdec": _fm_vec(tdec), "tfirst": _fm_vec(tfirst),
        "tshift": _fm_vec(np.asarray(tshiftv, np.float32).reshape(-1)),
        "ln1g": _fm_vec(g1), "ln1b": _fm_vec(b1),
        "ln2g": _fm_vec(g2), "ln2b": _fm_vec(b2),
        "b1v": _fm_vec(fb1), "b2v": _fm_vec(fb2),
        "ident": np.eye(128, dtype=np.float32),
        "onesc": np.ones((128, 1), np.float32),
        "onesr": np.ones((1, 128), np.float32),
        "nramp": np.ascontiguousarray(
            np.broadcast_to(-(np.arange(T, dtype=np.float32) + 1.0), (128, T))),
    }


def kernel(L0_attn_x, L0_attn_time_decay, L0_attn_time_first, L0_attn_time_shift,
           L0_attn_Wk, L0_attn_Wv, L0_attn_Wr, L0_attn_Wck, L0_attn_Wcv, L0_attn_Wcr,
           L0_attn_ln1_g, L0_attn_ln1_b, L0_attn_ln2_g, L0_attn_ln2_b,
           L0_ffn_w1, L0_ffn_b1, L0_ffn_w2, L0_ffn_b2):
    x = np.ascontiguousarray(np.asarray(L0_attn_x, dtype=np.float32))
    nc = _get_program(16)
    common = make_common(L0_attn_Wk, L0_attn_Wv, L0_attn_Wr, L0_attn_Wck, L0_attn_Wcv,
                         L0_attn_Wcr, L0_ffn_w1, L0_ffn_w2, L0_attn_time_decay,
                         L0_attn_time_first, L0_attn_time_shift, L0_attn_ln1_g,
                         L0_attn_ln1_b, L0_attn_ln2_g, L0_attn_ln2_b, L0_ffn_b1,
                         L0_ffn_b2)
    in_maps = [dict(common, x=np.ascontiguousarray(x[b])) for b in range(B)]
    res = run_bass_kernel_spmd(nc, in_maps, core_ids=list(range(B)))
    return np.stack([np.asarray(res.results[b]["y"]) for b in range(B)], axis=0)


# version 07
# speedup vs baseline: 1.3366x; 1.0688x over previous
import numpy as np
import ml_dtypes
import concourse.bass as bass
import concourse.mybir as mybir
from concourse.bass_utils import run_bass_kernel_spmd

F32 = mybir.dt.float32
FP8 = mybir.dt.float8e4
FP16 = mybir.dt.float16
OP = mybir.AluOpType
AF = mybir.ActivationFunctionType

B, T, C, H = 8, 8192, 1024, 4096
NCH = C // 128
NHH = H // 128
LN_EPS = 1e-5
NEG_BIG = -1e38
TO = 512
TOC = TO // 128
V_, S_, T_, SY, GP = "vector", "scalar", "tensor", "sync", "gpsimd"


class Sched:
    def __init__(self):
        self.items = []
        self.val = {}

    def emit(self, eng, fn, wait=(), inc=None, inc_n=1):
        w = [t for t in wait if t is not None]
        tok = None
        if inc is not None:
            self.val[inc] = self.val.get(inc, 0) + inc_n
            tok = (inc, self.val[inc])
        self.items.append((eng, fn, w, (inc, inc_n) if inc else None))
        return tok

    def replay(self, engine_name, sems, eng):
        last_wait = {}
        for (e, fn, waits, inc) in self.items:
            if e != engine_name:
                continue
            for (sem, val) in waits:
                if last_wait.get(sem, -1) >= val:
                    continue
                last_wait[sem] = val
                eng.wait_ge(sems[sem], val)
            ins = fn()
            if inc is not None and ins is not None:
                ins.then_inc(sems[inc[0]], inc[1])


def build_program(n_tiles=16, dbg=False):
    nc = bass.Bass("TRN2", target_bir_lowering=False,
                   detect_race_conditions=False)

    def ein(name, shape, dt=F32):
        return nc.dram_tensor(name, shape, dt, kind="ExternalInput").ap()

    x_in = ein("x", [T, C])
    Wk = ein("Wk", [C, C]); Wv = ein("Wv", [C, C]); Wr = ein("Wr", [C, C])
    Wck = ein("Wck", [C, H]); Wcv = ein("Wcv", [H, C]); Wcr = ein("Wcr", [C, C])
    w1 = ein("w1", [H, C]); w2 = ein("w2", [C, H])
    tdec = ein("tdec", [128, NCH]); tfirst = ein("tfirst", [128, NCH])
    tshift = ein("tshift", [128, NCH])
    ln1g = ein("ln1g", [128, NCH]); ln1b = ein("ln1b", [128, NCH])
    ln2g = ein("ln2g", [128, NCH]); ln2b = ein("ln2b", [128, NCH])
    b1v = ein("b1v", [128, NHH]); b2v = ein("b2v", [128, NCH])
    ident = ein("ident", [128, 128])
    onesc = ein("onesc", [128, 1])
    onesr = ein("onesr", [1, 128])
    nramp = ein("nramp", [128, T])

    y_out = nc.dram_tensor("y", [T, C], F32, kind="ExternalOutput").ap()

    def scr(name, shape, dt=F32):
        kind = "ExternalOutput" if dbg else "Internal"
        return nc.dram_tensor(name, shape, dt, kind=kind).ap()

    w1T = scr("w1T", [C, H])
    wqT = scr("wqT", [H, C], FP8)
    s_xn = scr("s_xn", [n_tiles, 128, NCH, TO])
    s_acc = scr("s_acc", [n_tiles * TO, C])
    s_v = scr("s_v", [n_tiles, 128, NCH, TO])
    s_rp = scr("s_rp", [n_tiles, 128, NCH, TO])
    s_x1 = scr("s_x1", [n_tiles, 128, NCH, TO])
    s_q1 = scr("s_q1", [n_tiles, 128, NCH, TO])

    sb_used = [0]
    def sb(name, cols, dt=F32):
        t = nc.alloc_sbuf_tensor(name, [128, cols], dt)
        sb_used[0] += cols * mybir.dt.size(dt)
        return t

    wcol = sb("wcol", NCH); ucol = sb("ucol", NCH); tscol = sb("tscol", NCH)
    g1col = sb("g1col", NCH); b1col = sb("b1col", NCH)
    g2col = sb("g2col", NCH); b2col = sb("b2col", NCH)
    fb1 = sb("fb1", NHH); bqcol = sb("bqcol", NCH); b2ft = sb("b2ft", NCH)
    identt = sb("identt", 128)
    onesct = nc.alloc_sbuf_tensor("onesct", [128, 1], F32)
    onesrt = nc.alloc_sbuf_tensor("onesrt", [1, 128], F32)
    sb_used[0] += 2 * 128 * 4
    spA = sb("spA", NCH); spB = sb("spB", NCH)
    mzC = sb("mzC", NCH); khC = sb("khC", NCH)
    rcar = sb("rcar", NCH); hcar = sb("hcar", NCH); h2car = sb("h2car", NCH)
    dcs = sb("dcs", NCH); lps = sb("lps", NCH)
    b1s = sb("b1s", NCH); nus = sb("nus", NCH)
    tmp8a = sb("tmp8a", NCH); tmp8b = sb("tmp8b", NCH)
    junk = sb("junk", 8)
    stat = sb("stat", 64)
    stg = sb("stg", 128)
    nrp = sb("nrp", TO)
    bcm = sb("bcm", TO); bcr = sb("bcr", TO)
    srow = nc.alloc_sbuf_tensor("srow", [1, 3 * TO], F32)
    sb_used[0] += 3 * TO * 4

    AH = sb("AH", NCH * (TO + 1))
    AKH = sb("AKH", NCH * TO)
    AAB = sb("AAB", NCH * TO)
    AABB = sb("AABB", NCH * TO)
    AZ = sb("AZ", NCH * (TO + 1))
    AZB = sb("AZB", NCH * (TO + 1))
    AKK = sb("AKK", 2 * NCH * TO)
    AWA = sb("AWA", NCH * C)
    AX = sb("AX", C)
    ASQ = sb("ASQ", C)
    ABN = sb("ABN", 2 * C)
    AVC = sb("AVC", C)
    APC = sb("APC", C)
    FQF = sb("FQF", NCH * TO, FP16)
    assert sb_used[0] <= 221 * 1024, f"SBUF over: {sb_used[0]/1024:.1f} KiB/part"

    psA = nc.alloc_psum_tensor("psA", [128, 512], F32)
    psC = nc.alloc_psum_tensor("psC", [128, 512], F32)
    psB = nc.alloc_psum_tensor("psB", [128, 512], F32)
    psT = nc.alloc_psum_tensor("psT", [128, 512], F32)
    psU = nc.alloc_psum_tensor("psU", [128, 512], F32)
    psS = nc.alloc_psum_tensor("psS", [128, 512], F32)
    psV = nc.alloc_psum_tensor("psV", [128, 512], F32)
    psW = nc.alloc_psum_tensor("psW", [128, 512], F32)

    S = Sched()
    vec, sca, ten = nc.vector, nc.scalar, nc.tensor
    LU = {}

    NLANES_LD, NLANES_ST = 12, 6
    lane_state = {"ld": 0, "st": 0}

    def ld(dst_ap, src_ap, wait=()):
        lane = lane_state["ld"]; lane_state["ld"] = (lane + 1) % NLANES_LD
        sem = f"sLD{lane}"
        prev = S.val.get(sem, 0)
        return S.emit(SY, lambda d=dst_ap, s=src_ap: nc.sync.dma_start(out=d, in_=s),
                      wait=list(wait) + [(sem, prev)], inc=sem, inc_n=16)

    def st(dst_ap, src_ap, wait=()):
        lane = lane_state["st"]; lane_state["st"] = (lane + 1) % NLANES_ST
        sem = f"sST{lane}"
        prev = S.val.get(sem, 0)
        return S.emit(GP, lambda d=dst_ap, s=src_ap: nc.gpsimd.dma_start(out=d, in_=s),
                      wait=list(wait) + [(sem, prev)], inc=sem, inc_n=16)

    def v(fn, wait=(), inc="sV"):
        return S.emit(V_, fn, wait=wait, inc=inc)

    def s(fn, wait=(), inc="sS"):
        return S.emit(S_, fn, wait=wait, inc=inc)

    def t(fn, wait=(), inc="sT"):
        return S.emit(T_, fn, wait=wait, inc=inc)

    def vsp():
        S.emit(V_, lambda: vec.tensor_scalar_mul(junk.ap(), junk.ap(), 1.0),
               wait=(), inc=None)

    def sigc(dst_ap, src_ap, tmp_ap, wait_first=(), wait_last=()):
        c1 = v(lambda: vec.tensor_single_scalar(tmp_ap, src_ap, 40.0, OP.min),
               wait=list(wait_first))
        s(lambda: sca.activation(tmp_ap, tmp_ap, AF.Exp), wait=[c1], inc=None)
        c3 = s(lambda: sca.activation(tmp_ap, tmp_ap, AF.Ln, bias=1.0))
        v(lambda: vec.tensor_tensor(tmp_ap, tmp_ap, src_ap, OP.max),
          wait=[c3], inc=None)
        c5 = v(lambda: vec.tensor_tensor(tmp_ap, src_ap, tmp_ap, OP.subtract))
        c6 = s(lambda: sca.activation(dst_ap, tmp_ap, AF.Exp),
               wait=[c5] + list(wait_last))
        return c6, c5

    def abn_wait():
        return [LU.get("bnc0"), LU.get("bnc1"), LU.get(("qch", 0)),
                LU.get(("qch", 1))]

    def abn_set(tok):
        LU["bnc0"] = tok; LU["bnc1"] = tok
        LU[("qch", 0)] = tok; LU[("qch", 1)] = tok

    mmb_state = {"i": 0}

    def slc_wait(sp):
        if sp == 0:
            return [LU.get("AVC"), LU.get(("vch", 0)), LU.get(("vch", 1))]
        return [LU.get("APC"), LU.get(("pch", 0)), LU.get(("pch", 1))]

    def slc_set(sp, tok):
        if sp == 0:
            LU["AVC"] = tok; LU[("vch", 0)] = tok; LU[("vch", 1)] = tok
        else:
            LU["APC"] = tok; LU[("pch", 0)] = tok; LU[("pch", 1)] = tok

    def next_mmbank():
        b = ((psA, "psA"), (psB, "psB"), (psC, "psC"))[mmb_state["i"] % 3]
        mmb_state["i"] += 1
        return b

    hb3 = AH.ap().rearrange("p (c t) -> p c t", c=NCH)
    kh3 = AKH.ap().rearrange("p (c t) -> p c t", c=NCH)
    abP = [AAB.ap().rearrange("p (c t) -> p c t", c=NCH),
           AABB.ap().rearrange("p (c t) -> p c t", c=NCH)]
    zfP = [AZ.ap().rearrange("p (c t) -> p c t", c=NCH),
           AZB.ap().rearrange("p (c t) -> p c t", c=NCH)]
    hq3 = AKK.ap().bitcast(FP8)[:, 0:NHH * TO].rearrange("p (c t) -> p c t", c=NHH)
    rb3 = AKK.ap()[:, 0:NCH * (TO + 1)].rearrange("p (c t) -> p c t", c=NCH)
    kv3 = AKK.ap()[:, NCH * TO:2 * NCH * TO].rearrange("p (c t) -> p c t", c=NCH)
    wcv4 = AKK.ap()[:, 0:4 * C].rearrange("p (h c) -> p h c", h=4)
    kkg3 = ABN.ap().rearrange("p (h t) -> p h t", h=4)
    fq3 = FQF.ap().rearrange("p (c t) -> p c t", c=NCH)
    xnf3 = AH.ap()[:, 0:NCH * TO].rearrange("p (c t) -> p c t", c=NCH)
    wS3 = [AWA.ap()[:, 0:NCH * TO].rearrange("p (c t) -> p c t", c=NCH),
           AWA.ap()[:, NCH * TO:2 * NCH * TO].rearrange("p (c t) -> p c t", c=NCH)]
    wck3P = [AVC.ap().rearrange("p (dh q) -> p dh q", q=128),
             APC.ap().rearrange("p (dh q) -> p dh q", q=128)]
    wck3 = wck3P[0]
    vch_h = [AVC.ap()[:, 0:TO], AVC.ap()[:, TO:2 * TO]]
    pch_h = [APC.ap()[:, 0:TO], APC.ap()[:, TO:2 * TO]]
    qch_h = [ABN.ap()[:, C:C + TO], ABN.ap()[:, C + TO:2 * C]]
    bnc_h = [ABN.ap()[:, 0:C], ABN.ap()[:, C:2 * C]]
    xch_h = [AX.ap()[:, 0:C], AX.ap()[:, 0:C]]
    pch8 = APC.ap().bitcast(FP8)[:, 0:H]
    w8P = [pch8.rearrange("p (hh q) -> p hh q", q=128),
           AVC.ap().bitcast(FP8)[:, 0:H].rearrange("p (hh q) -> p hh q", q=128)]
    wcv4P = [wcv4,
             AH.ap()[:, 0:4 * C].rearrange("p (h c) -> p h c", h=4)]

    v(lambda: vec.memset(junk.ap(), 0.0), inc=None)
    tok_id = ld(identt.ap(), ident)
    tok_oc = ld(onesct.ap(), onesc)
    tok_or = ld(onesrt.ap(), onesr)
    cvl = {}
    for (nm, dst, src) in (("w", wcol, tdec), ("u", ucol, tfirst), ("ts", tscol, tshift),
                           ("g1", g1col, ln1g), ("b1", b1col, ln1b),
                           ("g2", g2col, ln2g), ("b2", b2col, ln2b)):
        cvl[nm] = ld(dst.ap(), src)
    tok_b1 = ld(fb1.ap(), b1v)
    tok_b2f = ld(b2ft.ap(), b2v)
    c1 = v(lambda: vec.tensor_copy(tmp8a.ap().bitcast(FP16)[:, 0:NCH], b2ft.ap()),
           wait=[tok_b2f])
    vsp()
    tok_bq = v(lambda: vec.tensor_copy(bqcol.ap(), tmp8a.ap().bitcast(FP16)[:, 0:NCH]),
               wait=[c1])

    pp_toks = []
    pp_tok = None
    for hb in range(NHH):
        wload = ld(AX.ap()[:, 0:C], w1.rearrange("(hb p) c -> hb p c", p=128)[hb],
                   wait=[LU.get("AX"), pp_tok])
        ev = None
        for dc in range(NCH):
            pb = psT if dc % 2 == 0 else psV
            pbk = "psT" if dc % 2 == 0 else "psV"
            tk = t(lambda dc=dc, pb=pb: ten.transpose(
                pb.ap()[:, 0:128],
                AX.ap()[:, 128 * dc:128 * dc + 128], identt.ap()),
                wait=[wload, tok_id, LU.get(pbk)])
            ev = v(lambda dc=dc, pb=pb: vec.tensor_copy(
                ABN.ap()[:, 128 * dc:128 * dc + 128], pb.ap()[:, 0:128]),
                wait=[tk, LU.get("bnc0")])
            LU[pbk] = ev
        LU["AX"] = ev
        pp_tok = st(w1T.rearrange("(cb p) h -> p cb h", p=128)[:, :, 128 * hb:128 * hb + 128],
                    ABN.ap()[:, 0:C].rearrange("p (cb q) -> p cb q", q=128),
                    wait=[ev])
        pp_toks.append(pp_tok)
        LU["bnc0"] = pp_tok
        LU[("xch", 0)] = ev
        LU[("xch", 1)] = ev
    for cb in range(NCH):
        wload = ld(AWA.ap()[:, 0:H], w2.rearrange("(cb p) h -> cb p h", p=128)[cb],
                   wait=[LU.get(("AWS", 0)), LU.get(("AWS", 1)), pp_tok])
        ev = None
        for hb in range(NHH):
            pb = psU if hb % 2 == 0 else psW
            pbk = "psU" if hb % 2 == 0 else "psW"
            tk = t(lambda hb=hb, pb=pb: ten.transpose(
                pb.ap()[:, 0:128],
                AWA.ap()[:, 128 * hb:128 * hb + 128], identt.ap()),
                wait=[wload, tok_id, LU.get(pbk)])
            ev = v(lambda hb=hb, pb=pb: vec.tensor_copy(
                ABN.ap().bitcast(FP8)[:, 128 * hb:128 * hb + 128],
                pb.ap()[:, 0:128]),
                wait=[tk, LU.get("bnc0")])
            LU[pbk] = ev
        LU[("AWS", 0)] = ev
        LU[("AWS", 1)] = ev
        pp_tok = st(wqT.rearrange("(hb p) c -> p hb c", p=128)[:, :, 128 * cb:128 * cb + 128],
                    ABN.ap().bitcast(FP8)[:, 0:H].rearrange("p (hb q) -> p hb q", q=128),
                    wait=[ev])
        pp_toks.append(pp_tok)
        LU["bnc0"] = pp_tok

    v(lambda: vec.memset(spA.ap(), NEG_BIG), wait=[tok_bq])
    v(lambda: vec.memset(spB.ap(), NEG_BIG))
    v(lambda: vec.memset(mzC.ap(), 0.0))
    v(lambda: vec.memset(khC.ap(), 0.0))
    v(lambda: vec.memset(tmp8b.ap(), 0.0))
    v(lambda: vec.memset(rcar.ap(), 0.0))
    v(lambda: vec.tensor_copy(hcar.ap(), tscol.ap()), wait=[cvl["ts"]])
    init_tok = v(lambda: vec.tensor_copy(h2car.ap(), tscol.ap()))
    init_all = [init_tok, tok_b1, tok_oc, tok_or, tok_id] + list(cvl.values())

    x3 = x_in.rearrange("(n p) c -> n p c", p=128)
    acc3 = s_acc.rearrange("(n p) c -> n p c", p=128)
    y3 = y_out.rearrange("(n p) c -> n p c", p=128)

    state = {"first": True, "final_tok": None}

    def A_items(i, TD):
        TD["xn_toks"] = []
        TD["acc_toks"] = []
        items = []

        def mk_ch(ch):
            def go():
                g = i * TOC + ch
                half = ch % 2
                xs = xch_h[half]
                tok_x = ld(xs, x3[g], wait=[LU.get(("xch", 0)), LU.get(("xch", 1))])
                c0 = 16 * ch
                st_ = stat.ap()
                v(lambda xs=xs, c0=c0, st_=st_: vec.reduce_sum(
                    st_[:, c0:c0 + 1], xs, axis=mybir.AxisListType.X),
                    wait=[tok_x], inc=None)
                t_sq = s(lambda xs=xs: sca.activation(ASQ.ap(), xs, AF.Square),
                         wait=[tok_x, LU.get("ASQ")])
                v(lambda c0=c0, st_=st_: vec.reduce_sum(
                    st_[:, c0 + 1:c0 + 2], ASQ.ap(), axis=mybir.AxisListType.X),
                    wait=[t_sq], inc=None)
                v(lambda c0=c0, st_=st_: vec.tensor_scalar_mul(
                    st_[:, c0:c0 + 1], st_[:, c0:c0 + 1], 1.0 / C), inc=None)
                v(lambda c0=c0, st_=st_: vec.tensor_scalar_mul(
                    st_[:, c0 + 1:c0 + 2], st_[:, c0 + 1:c0 + 2], 1.0 / C), inc=None)
                v(lambda c0=c0, st_=st_: vec.tensor_tensor(
                    st_[:, c0 + 2:c0 + 3], st_[:, c0:c0 + 1], st_[:, c0:c0 + 1],
                    OP.mult), inc=None)
                vsp()
                v(lambda c0=c0, st_=st_: vec.tensor_tensor(
                    st_[:, c0 + 2:c0 + 3], st_[:, c0 + 1:c0 + 2], st_[:, c0 + 2:c0 + 3],
                    OP.subtract), inc=None)
                vsp()
                t_var = v(lambda c0=c0, st_=st_: vec.tensor_single_scalar(
                    st_[:, c0 + 2:c0 + 3], st_[:, c0 + 2:c0 + 3], LN_EPS, OP.add))
                t_ln = s(lambda c0=c0, st_=st_: sca.activation(
                    st_[:, c0 + 2:c0 + 3], st_[:, c0 + 2:c0 + 3], AF.Ln),
                    wait=[t_var])
                t_rs = s(lambda c0=c0, st_=st_: sca.activation(
                    st_[:, c0 + 2:c0 + 3], st_[:, c0 + 2:c0 + 3], AF.Exp, scale=-0.5),
                    wait=[t_ln])
                t_xn = v(lambda xs=xs, c0=c0, st_=st_: vec.tensor_scalar(
                    ASQ.ap(), xs, st_[:, c0:c0 + 1], st_[:, c0 + 2:c0 + 3],
                    OP.subtract, OP.mult), wait=[t_rs])
                tk_acc = st(acc3[g], xs, wait=[t_xn, LU.get(("xch", half))])
                TD["acc_toks"].append(tk_acc)
                LU["bnc0"] = tk_acc
                v(lambda c0=c0, st_=st_: vec.reduce_sum(
                    st_[:, c0 + 3:c0 + 4], ASQ.ap(), axis=mybir.AxisListType.X),
                    wait=[t_xn], inc=None)
                t_sq2 = s(lambda: sca.activation(bnc_h[1], ASQ.ap(), AF.Square),
                          wait=[t_xn, LU.get("bnc1"), LU.get(("qch", 0)),
                                LU.get(("qch", 1))])
                t_sxx2 = v(lambda c0=c0, st_=st_: vec.reduce_sum(
                    st_[:, c0 + 4:c0 + 5], bnc_h[1], axis=mybir.AxisListType.X),
                    wait=[t_sq2])
                LU["bnc1"] = t_sxx2
                LU[("qch", 0)] = t_sxx2
                LU[("qch", 1)] = t_sxx2
                v(lambda c0=c0, st_=st_: vec.tensor_scalar_mul(
                    st_[:, c0 + 3:c0 + 4], st_[:, c0 + 3:c0 + 4], 1.0 / C), inc=None)
                v(lambda c0=c0, st_=st_: vec.tensor_scalar_mul(
                    st_[:, c0 + 4:c0 + 5], st_[:, c0 + 4:c0 + 5], 1.0 / C), inc=None)
                v(lambda c0=c0, st_=st_: vec.tensor_tensor(
                    st_[:, c0 + 5:c0 + 6], st_[:, c0 + 3:c0 + 4], st_[:, c0 + 3:c0 + 4],
                    OP.mult), inc=None)
                vsp()
                v(lambda c0=c0, st_=st_: vec.tensor_tensor(
                    st_[:, c0 + 5:c0 + 6], st_[:, c0 + 4:c0 + 5], st_[:, c0 + 5:c0 + 6],
                    OP.subtract), inc=None)
                vsp()
                t_var2 = v(lambda c0=c0, st_=st_: vec.tensor_single_scalar(
                    st_[:, c0 + 5:c0 + 6], st_[:, c0 + 5:c0 + 6], LN_EPS, OP.add))
                t_ln2 = s(lambda c0=c0, st_=st_: sca.activation(
                    st_[:, c0 + 5:c0 + 6], st_[:, c0 + 5:c0 + 6], AF.Ln),
                    wait=[t_var2])
                t_rs2 = s(lambda c0=c0, st_=st_: sca.activation(
                    st_[:, c0 + 5:c0 + 6], st_[:, c0 + 5:c0 + 6], AF.Exp,
                    scale=-0.5), wait=[t_ln2])
                t_h = v(lambda xs=xs, c0=c0, st_=st_: vec.tensor_scalar(
                    xs, ASQ.ap(), st_[:, c0 + 3:c0 + 4], st_[:, c0 + 5:c0 + 6],
                    OP.subtract, OP.mult), wait=[t_rs2, tk_acc])
                ev_xn = None
                tk1_last = None
                hlast = None
                for j in range(NCH):
                    pb1 = psT if j % 2 == 0 else psV
                    pk1 = "psT" if j % 2 == 0 else "psV"
                    pb2 = psU if j % 2 == 0 else psW
                    pk2 = "psU" if j % 2 == 0 else "psW"
                    tk1 = t(lambda j=j, pb1=pb1: ten.transpose(
                        pb1.ap()[:, 0:128],
                        ASQ.ap()[:, 128 * j:128 * j + 128], identt.ap()),
                        wait=[t_xn, tok_id, LU.get(pk1)])
                    tk1_last = tk1
                    ev_xn = v(lambda j=j, pb1=pb1: vec.tensor_copy(
                        ABN.ap()[:, 128 * j:128 * j + 128],
                        pb1.ap()[:, 0:128]),
                        wait=[tk1, LU.get("bnc0")])
                    LU[pk1] = ev_xn
                    tk2 = t(lambda j=j, pb2=pb2, xs=xs: ten.transpose(
                        pb2.ap()[:, 0:128],
                        xs[:, 128 * j:128 * j + 128], identt.ap()),
                        wait=[t_h, tok_id, LU.get(pk2)])
                    hlast = v(lambda j=j, pb2=pb2, ch=ch: vec.tensor_scalar(
                        hb3[:, j, 1 + 128 * ch:1 + 128 * ch + 128],
                        pb2.ap()[:, 0:128],
                        g1col.ap()[:, j:j + 1], b1col.ap()[:, j:j + 1],
                        OP.mult, OP.add),
                        wait=[tk2, LU.get("AH")] + init_all)
                    LU[pk2] = hlast
                LU[("xch", half)] = hlast
                tk_xn = st(s_xn[i, :, :, 128 * ch:128 * ch + 128],
                           ABN.ap()[:, 0:C].rearrange("p (c q) -> p c q", c=NCH),
                           wait=[ev_xn])
                TD["xn_toks"].append(tk_xn)
                LU["bnc0"] = tk_xn
                LU["ASQ"] = tk1_last
                TD["_hlast"] = hlast
            return go
        for ch in range(TOC):
            items.append(mk_ch(ch))

        def it_lead():
            t_lead = v(lambda: vec.tensor_copy(hb3[:, :, 0:1].squeeze(2), hcar.ap()),
                       wait=[LU.get("AH")] + init_all)
            t_car = v(lambda: vec.tensor_copy(hcar.ap(), hb3[:, :, TO:TO + 1].squeeze(2)),
                      wait=[t_lead])
            TD["hready"] = [TD["_hlast"], t_lead, t_car]
        items.append(it_lead)
        return items

    def B_items(i, TD):
        TD["vr_toks"] = []
        items = []
        tbase = i * TO

        def mk_wh(idx, kind, Wsrc, half):
            def go():
                if kind == "k" and half == 0:
                    TD["tok_nr"] = ld(nrp.ap(), nramp[:, tbase:tbase + TO],
                                      wait=[LU.get("nrp")])
                sl = (2 * idx + half) % 2
                wtok = ld(wS3[sl], Wsrc.rearrange("(dh p) c -> p dh c", p=128)
                          [:, :, 512 * half:512 * half + 512],
                          wait=[LU.get(("AWS", sl))])
                mm_last = None
                for mi in range(4 * half, 4 * half + 4):
                    ps, bank = next_mmbank()
                    mm = None
                    for ki in range(NCH):
                        mm = t(lambda ki=ki, mi=mi, ps=ps, sl=sl, half=half:
                               ten.matmul(
                                   ps.ap(),
                                   wS3[sl][:, ki, 128 * (mi - 4 * half):
                                           128 * (mi - 4 * half) + 128],
                                   hb3[:, ki, 0:TO],
                                   start=(ki == 0), stop=(ki == NCH - 1)),
                               wait=[wtok, LU.get(bank)]
                               + (TD["hready"] if ki == 0 else []))
                    mm_last = mm
                    if kind == "k":
                        ev = v(lambda mi=mi, ps=ps: vec.scalar_tensor_tensor(
                            kh3[:, mi, 0:TO], nrp.ap(),
                            wcol.ap()[:, mi:mi + 1], ps.ap(), OP.mult, OP.add),
                            wait=[mm, TD["tok_nr"], LU.get("AKH")])
                        TD["kh_ready"] = ev
                    else:
                        bh = mi % 2
                        ev = v(lambda ps=ps, bh=bh: vec.tensor_copy(
                            bnc_h[bh][:, 0:TO], ps.ap()),
                            wait=[mm, LU.get("bnc" + str(bh))])
                        dst = s_v if kind == "v" else s_rp
                        tk2 = st(dst[i, :, mi, :], bnc_h[bh][:, 0:TO], wait=[ev])
                        TD["vr_toks"].append(tk2)
                        LU["bnc" + str(bh)] = tk2
                    LU[bank] = ev
                LU[("AWS", sl)] = mm_last
                LU["AH"] = mm_last
                if kind == "k" and half == 1:
                    LU["nrp"] = TD["kh_ready"]
            return go
        for idx, (kind, Wsrc) in enumerate([("k", Wk), ("v", Wv), ("r", Wr)]):
            for half in range(2):
                items.append(mk_wh(idx, kind, Wsrc, half))
        return items

    def stage_C(i, TD, interleave=()):
        q = i % 2
        zb3 = zfP[q]
        ab3 = abP[q]
        kh_ready = TD["kh_ready"]
        inter = list(interleave)
        if state["first"]:
            v(lambda: vec.memset(zb3[:, :, 0:1].squeeze(2), NEG_BIG),
              wait=[kh_ready, LU.get(("AZ", q))], inc=None)
            vsp()
        else:
            v(lambda: vec.tensor_tensor(
                tmp8b.ap(), khC.ap(), kh3[:, :, 0:1].squeeze(2), OP.subtract),
                wait=[kh_ready], inc=None)
            vsp()
            v(lambda: vec.tensor_tensor(tmp8a.ap(), tmp8b.ap(), mzC.ap(),
                                        OP.subtract), inc=None)
            vsp()
            v(lambda: vec.scalar_tensor_tensor(
                zb3[:, :, 0:1].squeeze(2), spA.ap(), 2.0, tmp8a.ap(),
                OP.mult, OP.add),
                wait=[LU.get(("AZ", q))], inc=None)
            vsp()
        v(lambda: vec.tensor_copy(khC.ap(), kh3[:, :, TO - 1:TO].squeeze(2)),
          inc=None)
        pro_last = None
        for j in range(NCH):
            pro_last = v(lambda j=j: vec.tensor_tensor(
                ab3[:, j, 0:TO - 1], kh3[:, j, 0:TO - 1], kh3[:, j, 1:TO],
                OP.subtract),
                wait=[LU.get(("AAB", q))])
        LU["AKH"] = pro_last
        v(lambda: vec.tensor_tensor(b1s.ap(), spB.ap(), tmp8b.ap(), OP.add),
          inc=None)
        vsp()
        n_it = len(inter)
        stride = max(1, (TO - 24) // n_it) if n_it else TO
        sp_old, sp_new = spB, spA
        chain_tok = None
        for tl in range(TO):
            zsl = zb3[:, :, tl:tl + 1].squeeze(2)
            v(lambda zsl=zsl: vec.tensor_single_scalar(dcs.ap(), zsl, 40.0, OP.min),
              inc="sCV")
            cv_val = S.val["sCV"]
            s(lambda: sca.activation(lps.ap(), dcs.ap(), AF.Exp),
              wait=[("sCV", cv_val)], inc=None)
            s(lambda: sca.activation(lps.ap(), lps.ap(), AF.Ln, bias=1.0), inc="sCS")
            cs_val = S.val["sCS"]
            if tl < TO - 1:
                v(lambda: vec.tensor_single_scalar(nus.ap(), b1s.ap(), 0.0, OP.max),
                  inc=None)
            inc_flag = "sV" if tl == TO - 1 else None
            tspn = v(lambda sp_new=sp_new, zsl=zsl: vec.tensor_tensor(
                sp_new.ap(), lps.ap(), zsl, OP.max),
                wait=[("sCS", cs_val)], inc=inc_flag)
            if tl < TO - 1:
                v(lambda tl=tl: vec.tensor_tensor(
                    nus.ap(), ab3[:, :, tl:tl + 1].squeeze(2), nus.ap(),
                    OP.subtract), inc=None)
                v(lambda sp_new=sp_new, tl=tl: vec.tensor_tensor(
                    b1s.ap(), sp_new.ap(), ab3[:, :, tl:tl + 1].squeeze(2),
                    OP.add), inc=None)
                v(lambda sp_new=sp_new, tl=tl: vec.scalar_tensor_tensor(
                    zb3[:, :, tl + 1:tl + 2].squeeze(2), sp_new.ap(), 2.0,
                    nus.ap(), OP.mult, OP.add), inc=None)
                vsp()
                if inter and tl % stride == 8 % stride:
                    inter.pop(0)()
            else:
                chain_tok = tspn
                vsp()
                v(lambda sp_old=sp_old: vec.tensor_tensor(
                    b1s.ap(), sp_old.ap(), ab3[:, :, TO - 2:TO - 1].squeeze(2),
                    OP.add), inc=None)
                vsp()
                v(lambda: vec.tensor_single_scalar(mzC.ap(), b1s.ap(), 0.0,
                                                   OP.max), inc=None)
            sp_old, sp_new = sp_new, sp_old
        if sp_old is not spA:
            chain_tok = v(lambda sp_old=sp_old: vec.tensor_copy(spA.ap(),
                                                                sp_old.ap()))
        for fn in inter:
            fn()
        state["first"] = False
        LU[("AAB", q)] = chain_tok
        TD["chain_tok"] = chain_tok

    def D_items(i, TD):
        p = i % 2
        zb3 = zfP[p]
        lu3 = abP[p]
        items = []
        TD["x1_sp"] = []

        def mk_j(j):
            def go():
                chain_tok = TD["chain_tok"]
                half = j % 2
                tv_ = ld(vch_h[half], s_v[i, :, j, :],
                         wait=TD["vr_toks"] + [LU.get(("vch", half)),
                                               LU.get("AVC")])
                tp_ = ld(pch_h[half], s_rp[i, :, j, :],
                         wait=TD["vr_toks"] + [LU.get(("pch", half)),
                                               LU.get("APC")])
                tx_ = ld(qch_h[half], s_xn[i, :, j, :],
                         wait=TD["xn_toks"] + [LU.get(("qch", half)),
                                               LU.get("bnc1")])
                t1 = v(lambda j=j: vec.tensor_tensor(
                    lu3[:, j, 0:TO], zb3[:, j, 0:TO],
                    ucol.ap()[:, j:j + 1].broadcast_to([128, TO]), OP.subtract),
                    wait=[chain_tok, LU.get(("AAB", p))])
                t2, _ = sigc(lu3[:, j, 0:TO], lu3[:, j, 0:TO], ASQ.ap()[:, 0:TO],
                             wait_first=[t1, LU.get("ASQ"), LU.get(("sqh", 0))])
                t3, _ = sigc(zb3[:, j, 0:TO], zb3[:, j, 0:TO], ASQ.ap()[:, 0:TO],
                             wait_first=[t1, t2])
                t4 = v(lambda j=j: vec.tensor_scalar(
                    bnc_h[0][:, 0:TO], zb3[:, j, 0:TO], -1.0, 1.0, OP.mult, OP.add),
                    wait=[t3, LU.get("bnc0")])
                v(lambda j=j: vec.tensor_copy(rb3[:, j, 0:1],
                                              rcar.ap()[:, j:j + 1]),
                  wait=[LU.get("AKK")], inc=None)
                t5 = v(lambda half=half: vec.tensor_tensor(
                    bnc_h[0][:, 0:TO], bnc_h[0][:, 0:TO], vch_h[half], OP.mult),
                    wait=[t4, tv_], inc=None)
                vsp()
                t7 = v(lambda j=j: vec.tensor_tensor_scan(
                    rb3[:, j, 1:TO + 1], zb3[:, j, 0:TO], bnc_h[0][:, 0:TO],
                    rcar.ap()[:, j:j + 1], OP.mult, OP.add), wait=[t5])
                LU["bnc0"] = t7
                vsp()
                v(lambda j=j: vec.tensor_copy(rcar.ap()[:, j:j + 1],
                                              rb3[:, j, TO:TO + 1]), inc=None)
                t9 = v(lambda j=j, half=half: vec.tensor_tensor(
                    bnc_h[0][:, 0:TO], rb3[:, j, 0:TO], vch_h[half], OP.subtract),
                    wait=[t7, tv_])
                LU["bnc0"] = t9
                vsp()
                v(lambda j=j: vec.tensor_tensor(
                    lu3[:, j, 0:TO], lu3[:, j, 0:TO], bnc_h[0][:, 0:TO], OP.mult),
                    wait=[t9, t2], inc=None)
                vsp()
                t11 = v(lambda j=j, half=half: vec.tensor_tensor(
                    lu3[:, j, 0:TO], lu3[:, j, 0:TO], vch_h[half], OP.add),
                    inc=None)
                t12, _ = sigc(pch_h[half], pch_h[half], ASQ.ap()[:, 0:TO],
                              wait_first=[tp_, t3])
                v(lambda j=j, half=half: vec.tensor_tensor(
                    pch_h[half], pch_h[half], lu3[:, j, 0:TO], OP.mult),
                    wait=[t12, t11], inc=None)
                vsp()
                t14 = v(lambda j=j, half=half: vec.tensor_tensor(
                    pch_h[half], pch_h[half], qch_h[half], OP.add), wait=[tx_])
                tk_sp = st(s_x1[i, :, j, :], pch_h[half], wait=[t14])
                TD["x1_sp"].append(tk_sp)
                LU[("pch", half)] = tk_sp
                LU[("qch", half)] = t14
                LU["bnc1"] = t14
                LU[("vch", half)] = t14
                LU["ASQ"] = t12
                LU[("sqh", 0)] = t12
                if j == NCH - 1:
                    LU[("AZ", p)] = tk_sp
                    LU[("AAB", p)] = tk_sp
                    LU["AKK"] = tk_sp
            return go
        for j in range(NCH):
            items.append(mk_j(j))
        return items

    def E_items(i, TD):
        p = i % 2
        lu3 = abP[p]
        h2f = zfP[p]
        items = []
        TD["q1_sp"] = []
        TD["kk_toks"] = {}

        def it_xstats():
            x1_toks = []
            for j in range(NCH):
                x1_toks.append(ld(lu3[:, j, 0:TO], s_x1[i, :, j, :],
                                  wait=TD["x1_sp"] + [LU.get(("AAB", p))]))
            TD["x1_toks"] = x1_toks
            acc = bnc_h[0][:, 0:TO]
            va = v(lambda: vec.tensor_tensor(acc, lu3[:, 0, 0:TO],
                                             lu3[:, 1, 0:TO], OP.add),
                   wait=[x1_toks[0], x1_toks[1], LU.get("bnc0")])
            for j in range(2, NCH):
                va = v(lambda j=j: vec.tensor_tensor(acc, acc, lu3[:, j, 0:TO],
                                                     OP.add),
                       wait=[x1_toks[j]])
            mm = t(lambda: ten.matmul(psS.ap()[0:1, 0:512], onesct.ap(), acc,
                                      start=True, stop=True),
                   wait=[va, tok_oc, LU.get("psS")])
            LU["bnc0"] = mm
            ev = v(lambda: vec.tensor_copy(srow.ap()[0:1, 0:TO],
                                           psS.ap()[0:1, 0:512]), wait=[mm])
            LU["psS"] = ev
            TD["_ev1"] = ev
        items.append(it_xstats)

        def it_sqstats():
            x1_toks = TD["x1_toks"]
            acc2 = qch_h[0][:, 0:TO]
            va2 = None
            for j in range(NCH):
                sh = j % 2
                tsq = s(lambda j=j, sh=sh: sca.activation(
                    ASQ.ap()[:, 512 * sh:512 * sh + 512],
                    lu3[:, j, 0:TO], AF.Square),
                    wait=[x1_toks[j], LU.get("ASQ"), LU.get(("sqh", sh))])
                if j == 0:
                    va2 = v(lambda sh=sh: vec.tensor_copy(
                        acc2, ASQ.ap()[:, 512 * sh:512 * sh + 512]),
                        wait=[tsq, LU.get(("qch", 0))])
                else:
                    va2 = v(lambda sh=sh: vec.tensor_tensor(
                        acc2, acc2, ASQ.ap()[:, 512 * sh:512 * sh + 512],
                        OP.add), wait=[tsq])
                LU[("sqh", sh)] = va2
            mm2 = t(lambda: ten.matmul(psS.ap()[0:1, 0:512], onesct.ap(), acc2,
                                       start=True, stop=True),
                    wait=[va2, tok_oc, LU.get("psS")])
            LU[("qch", 0)] = mm2
            ev2 = v(lambda: vec.tensor_copy(srow.ap()[0:1, TO:2 * TO],
                                            psS.ap()[0:1, 0:512]), wait=[mm2])
            LU["psS"] = ev2
            LU["ASQ"] = va2
            ev = TD["_ev1"]
            v(lambda: vec.tensor_scalar_mul(srow.ap()[0:1, 0:TO],
                                            srow.ap()[0:1, 0:TO], 1.0 / C),
              wait=[ev, ev2], inc=None)
            v(lambda: vec.tensor_scalar_mul(srow.ap()[0:1, TO:2 * TO],
                                            srow.ap()[0:1, TO:2 * TO], 1.0 / C),
              inc=None)
            v(lambda: vec.tensor_tensor(srow.ap()[0:1, 2 * TO:3 * TO],
                                        srow.ap()[0:1, 0:TO],
                                        srow.ap()[0:1, 0:TO], OP.mult), inc=None)
            vsp()
            v(lambda: vec.tensor_tensor(srow.ap()[0:1, 2 * TO:3 * TO],
                                        srow.ap()[0:1, TO:2 * TO],
                                        srow.ap()[0:1, 2 * TO:3 * TO],
                                        OP.subtract), inc=None)
            vsp()
            r4 = v(lambda: vec.tensor_single_scalar(srow.ap()[0:1, 2 * TO:3 * TO],
                                                    srow.ap()[0:1, 2 * TO:3 * TO],
                                                    LN_EPS, OP.add))
            r5 = s(lambda: sca.activation(srow.ap()[0:1, 2 * TO:3 * TO],
                                          srow.ap()[0:1, 2 * TO:3 * TO], AF.Ln),
                   wait=[r4])
            r6 = s(lambda: sca.activation(srow.ap()[0:1, 2 * TO:3 * TO],
                                          srow.ap()[0:1, 2 * TO:3 * TO], AF.Exp,
                                          scale=-0.5), wait=[r5])
            bc_last = None
            for (src_ofs, dstb) in ((0, bcm), (2 * TO, bcr)):
                mmb = t(lambda src_ofs=src_ofs: ten.matmul(
                    psS.ap()[:, 0:512], onesrt.ap(),
                    srow.ap()[0:1, src_ofs:src_ofs + TO],
                    start=True, stop=True),
                    wait=[r6, tok_or, LU.get("psS")])
                bc_last = v(lambda dstb=dstb: vec.tensor_copy(
                    dstb.ap(), psS.ap()[:, 0:512]),
                    wait=[mmb, LU.get("bcmr")])
                LU["psS"] = bc_last
            TD["bc_last"] = bc_last
        items.append(it_sqstats)

        def it_h2():
            bc_last = TD["bc_last"]
            x1_toks = TD["x1_toks"]
            for j in range(NCH):
                v(lambda j=j: vec.tensor_tensor(
                    h2f[:, j, 1:TO + 1], lu3[:, j, 0:TO], bcm.ap(), OP.subtract),
                    wait=[bc_last, x1_toks[j], LU.get(("AZ", p))], inc=None)
                vsp()
                v(lambda j=j: vec.tensor_tensor(
                    h2f[:, j, 1:TO + 1], h2f[:, j, 1:TO + 1], bcr.ap(), OP.mult),
                    inc=None)
                vsp()
                v(lambda j=j: vec.tensor_scalar(
                    h2f[:, j, 1:TO + 1], h2f[:, j, 1:TO + 1],
                    g2col.ap()[:, j:j + 1], b2col.ap()[:, j:j + 1],
                    OP.mult, OP.add), inc=None)
            v(lambda: vec.tensor_copy(h2f[:, :, 0:1].squeeze(2), h2car.ap()),
              wait=[LU.get("h2car_w")], inc=None)
            h2_last = v(lambda: vec.tensor_copy(h2car.ap(),
                                                h2f[:, :, TO:TO + 1].squeeze(2)))
            LU["h2car_w"] = h2_last
            LU["bcmr"] = h2_last
            TD["h2_last"] = h2_last
        items.append(it_h2)

        def mk_ck(g):
            def go():
                h2_last = TD["h2_last"]
                kk_toks = []
                for hkl in range(4):
                    hk = 4 * g + hkl
                    sp = hk % 2
                    wtk = ld(wck3P[sp],
                             Wck.rearrange("(dh p) h -> p dh h", p=128)
                             [:, :, 128 * hk:128 * hk + 128],
                             wait=slc_wait(sp))
                    ps, bank = next_mmbank()
                    mm = None
                    for ki in range(NCH):
                        mm = t(lambda ki=ki, ps=ps, sp=sp: ten.matmul(
                            ps.ap()[:, 0:TO], wck3P[sp][:, ki, :],
                            h2f[:, ki, 0:TO],
                            start=(ki == 0), stop=(ki == NCH - 1)),
                            wait=[wtk, LU.get(bank), h2_last])
                    slc_set(sp, mm)
                    sh = hkl % 2
                    e1 = s(lambda ps=ps, sh=sh: sca.activation(
                        ASQ.ap()[:, 512 * sh:512 * sh + 512], ps.ap()[:, 0:TO],
                        AF.Relu),
                        wait=[mm, LU.get(("sqh", sh)), LU.get("ASQ")])
                    e2 = s(lambda hkl=hkl, sh=sh: sca.activation(
                        kkg3[:, hkl, :], ASQ.ap()[:, 512 * sh:512 * sh + 512],
                        AF.Square),
                        wait=abn_wait())
                    LU[("sqh", sh)] = e2
                    LU[bank] = e1
                    kk_toks.append(e2)
                TD["kk_toks"][g] = kk_toks
            return go

        def mk_kv(g):
            def go():
                kk_toks = TD["kk_toks"][g]
                wp = g % 2
                wtk2 = ld(wcv4P[wp],
                          Wcv.rearrange("(g4 hkl p) c -> g4 p hkl c", p=128,
                                        hkl=4)[g],
                          wait=([LU.get("AKK"), LU.get("AKKst")] if wp == 0
                                else [LU.get("AH"), LU.get("AHst")]))
                g_mm_last = None
                for mi in range(NCH):
                    ps = psT if mi % 2 == 0 else psU
                    bank = "psT" if mi % 2 == 0 else "psU"
                    mm = None
                    for hkl in range(4):
                        mm = t(lambda hkl=hkl, mi=mi, ps=ps, wp=wp: ten.matmul(
                            ps.ap()[:, 0:TO],
                            wcv4P[wp][:, hkl, 128 * mi:128 * mi + 128],
                            kkg3[:, hkl, :],
                            start=(hkl == 0), stop=(hkl == 3)),
                            wait=[wtk2, kk_toks[hkl], LU.get(bank)])
                    g_mm_last = mm
                    if g == 0:
                        ev = v(lambda mi=mi, ps=ps: vec.tensor_copy(
                            kv3[:, mi, 0:TO], ps.ap()[:, 0:TO]),
                            wait=[mm, LU.get("AKKkv")])
                    else:
                        ev = v(lambda mi=mi, ps=ps: vec.tensor_tensor(
                            kv3[:, mi, 0:TO], kv3[:, mi, 0:TO], ps.ap()[:, 0:TO],
                            OP.add), wait=[mm])
                    LU[bank] = ev
                    if g == 7:
                        TD.setdefault("kv_evs", []).append(ev)
                abn_set(g_mm_last)
                if wp == 0:
                    LU["AKKst"] = g_mm_last
                    if g == 6:
                        LU["AKK"] = g_mm_last
                else:
                    LU["AHst"] = g_mm_last
                    LU["AH"] = g_mm_last
            return go
        for g in range(8):
            items.append(mk_ck(g))
            items.append(mk_kv(g))

        def it_cr_load():
            w0 = ld(wS3[0], Wcr.rearrange("(dh p) c -> p dh c", p=128)
                    [:, :, 0:512], wait=[LU.get(("AWS", 0))])
            w1_ = ld(wS3[1], Wcr.rearrange("(dh p) c -> p dh c", p=128)
                     [:, :, 512:1024], wait=[LU.get(("AWS", 1))])
            TD["cr_wtok"] = [w0, w1_]
        items.append(it_cr_load)

        def mk_cr(mi):
            def go():
                sl = mi // 4
                wtok = TD["cr_wtok"][sl]
                ps, bank = next_mmbank()
                mm = None
                for ki in range(NCH):
                    mm = t(lambda ki=ki, mi=mi, ps=ps, sl=sl: ten.matmul(
                        ps.ap(), wS3[sl][:, ki, 128 * (mi - 4 * sl):
                                         128 * (mi - 4 * sl) + 128],
                        h2f[:, ki, 0:TO],
                        start=(ki == 0), stop=(ki == NCH - 1)),
                        wait=[wtok, LU.get(bank), TD["h2_last"]])
                LU[("AWS", sl)] = mm
                e1, e1r = sigc(bnc_h[0][:, 0:TO], ps.ap()[:, 0:TO],
                               ASQ.ap()[:, 0:TO],
                               wait_first=[mm, LU.get("ASQ"),
                                           LU.get(("sqh", 0))],
                               wait_last=[LU.get("bnc0")])
                LU["ASQ"] = e1
                LU[("sqh", 0)] = e1
                LU[bank] = e1r
                v(lambda mi=mi: vec.tensor_tensor(
                    bnc_h[0][:, 0:TO], bnc_h[0][:, 0:TO], kv3[:, mi, 0:TO],
                    OP.mult),
                    wait=[e1] + TD.get("kv_evs", []), inc=None)
                vsp()
                e3 = v(lambda mi=mi: vec.tensor_tensor(
                    bnc_h[0][:, 0:TO], bnc_h[0][:, 0:TO], lu3[:, mi, 0:TO],
                    OP.add),
                    wait=[TD["x1_toks"][mi]])
                tk = st(s_q1[i, :, mi, :], bnc_h[0][:, 0:TO], wait=[e3])
                TD["q1_sp"].append(tk)
                LU["bnc0"] = tk
                if mi == NCH - 1:
                    LU[("AAB", p)] = tk
                    LU[("AZ", p)] = mm
                    LU["AKKkv"] = tk
            return go
        for mi in range(NCH):
            items.append(mk_cr(mi))
        return items

    def F1_items(i, TD):
        items = []
        st_ = {}

        def emit_xn_loads():
            toks = []
            for j in range(NCH):
                toks.append(ld(xnf3[:, j, :], s_xn[i, :, j, :],
                               wait=TD["xn_toks"] + [LU.get("AH")]))
            st_["xn"] = toks
        items.append(emit_xn_loads)

        def mk_hq(hk):
            def go():
                sp = hk % 2
                wtk = ld(wck3P[sp],
                         w1T.rearrange("(dh p) h -> p dh h", p=128)
                         [:, :, 128 * hk:128 * hk + 128],
                         wait=slc_wait(sp) + pp_toks)
                ps, bank = next_mmbank()
                mm = None
                for ki in range(NCH):
                    mm = t(lambda ki=ki, ps=ps, sp=sp: ten.matmul(
                        ps.ap(),
                        wck3P[sp][:, ki, :],
                        xnf3[:, ki, :],
                        start=(ki == 0), stop=(ki == NCH - 1)),
                        wait=[wtk, st_["xn"][ki], LU.get(bank)])
                slc_set(sp, mm)
                ev = s(lambda hk=hk, ps=ps: sca.activation(
                    hq3[:, hk, :], ps.ap(), AF.Relu, bias=fb1.ap()[:, hk:hk + 1]),
                    wait=[mm, tok_b1, LU.get("AKK")])
                LU[bank] = ev
                st_.setdefault("hq", []).append(ev)
                if hk == NHH - 1:
                    LU["AH"] = mm
            return go
        for hk in range(NHH):
            items.append(mk_hq(hk))

        def mk_fq(mi):
            def go():
                sp = mi % 2
                wtk = ld(w8P[sp],
                         wqT.rearrange("(hh p) c -> p hh c", p=128)
                         [:, :, 128 * mi:128 * mi + 128],
                         wait=slc_wait(1 - sp) + pp_toks)
                ps, bank = next_mmbank()
                mm = None
                for pg in range(NHH // 2):
                    mm = t(lambda pg=pg, ps=ps, sp=sp: ten.matmul(
                        ps.ap(),
                        w8P[sp][:, 2 * pg:2 * pg + 2, :],
                        hq3[:, 2 * pg:2 * pg + 2, :],
                        start=(pg == 0), stop=(pg == NHH // 2 - 1),
                        perf_mode=mybir.MatmulPerfMode.DoubleRow),
                        wait=[wtk, st_["hq"][2 * pg], st_["hq"][2 * pg + 1],
                              LU.get(bank)])
                slc_set(1 - sp, mm)
                ev = v(lambda mi=mi, ps=ps: vec.tensor_single_scalar(
                    fq3[:, mi, :], ps.ap(), bqcol.ap()[:, mi:mi + 1], OP.add),
                    wait=[mm, tok_bq, LU.get("FQF")])
                LU[bank] = ev
                TD.setdefault("fq_toks", []).append(ev)
                if mi == NCH - 1:
                    LU["AKK"] = mm
            return go
        for mi in range(NCH):
            items.append(mk_fq(mi))
        return items

    def F2_items(i, TD):
        items = []

        def mk_ch(ch):
            def go():
                g = i * TOC + ch
                ta = ld(xch_h[0], acc3[g],
                        wait=TD["acc_toks"] + [LU.get(("xch", 0)),
                                               LU.get(("xch", 1))])
                evs = []
                for j in range(NCH):
                    tq = ld(ABN.ap()[:, C + 128 * j:C + 128 * j + 128],
                            s_q1[i, :, j, 128 * ch:128 * ch + 128],
                            wait=TD["q1_sp"] + [LU.get(("qch", 0)),
                                                LU.get(("qch", 1)),
                                                LU.get("bnc1")])
                    t1 = v(lambda j=j, ch=ch: vec.tensor_tensor(
                        stg.ap(), ABN.ap()[:, C + 128 * j:C + 128 * j + 128],
                        fq3[:, j, 128 * ch:128 * ch + 128], OP.add),
                        wait=[tq] + TD["fq_toks"] + [LU.get("stg")])
                    pb1 = psT if j % 2 == 0 else psV
                    pk1 = "psT" if j % 2 == 0 else "psV"
                    tk = t(lambda pb1=pb1: ten.transpose(pb1.ap()[:, 0:128],
                                                         stg.ap(), identt.ap()),
                           wait=[t1, tok_id, LU.get(pk1)])
                    LU["stg"] = tk
                    ev = v(lambda j=j, pb1=pb1: vec.tensor_tensor(
                        bnc_h[0][:, 128 * j:128 * j + 128], pb1.ap()[:, 0:128],
                        xch_h[0][:, 128 * j:128 * j + 128], OP.add),
                        wait=[tk, ta, LU.get("bnc0")])
                    LU[pk1] = ev
                    evs.append(ev)
                out_tok = st(y3[g], bnc_h[0], wait=evs)
                LU["bnc0"] = out_tok
                LU[("xch", 0)] = out_tok
                LU[("xch", 1)] = out_tok
                LU[("qch", 0)] = out_tok
                LU[("qch", 1)] = out_tok
                LU["bnc1"] = out_tok
                if ch == TOC - 1:
                    LU["FQF"] = out_tok
                state["final_tok"] = out_tok
            return go
        for ch in range(TOC):
            items.append(mk_ch(ch))
        return items

    TD = {i: {} for i in range(n_tiles)}
    for fn in A_items(0, TD[0]) + B_items(0, TD[0]):
        fn()
    for i in range(n_tiles):
        items = []
        if i >= 1:
            items += F1_items(i - 1, TD[i - 1])
            items += D_items(i - 1, TD[i - 1])
            items += E_items(i - 1, TD[i - 1])
            items += F2_items(i - 1, TD[i - 1])
        if i + 1 < n_tiles:
            items += A_items(i + 1, TD[i + 1])
            items += B_items(i + 1, TD[i + 1])
        stage_C(i, TD[i], items)
    il = n_tiles - 1
    for fn in (F1_items(il, TD[il]) + D_items(il, TD[il])
               + E_items(il, TD[il]) + F2_items(il, TD[il])):
        fn()

    import contextlib
    stack = contextlib.ExitStack()
    sems = {}
    sem_names = (["sT", "sV", "sS", "sCV", "sCS"]
                 + [f"sLD{i}" for i in range(NLANES_LD)]
                 + [f"sST{i}" for i in range(NLANES_ST)])
    for name in sem_names:
        sems[name] = stack.enter_context(nc.semaphore(name))
    with nc.Block() as blk:
        @blk.sync
        def _(eng):
            S.replay(SY, sems, eng)

        @blk.tensor
        def _(eng):
            S.replay(T_, sems, eng)

        @blk.vector
        def _(eng):
            S.replay(V_, sems, eng)

        @blk.scalar
        def _(eng):
            S.replay(S_, sems, eng)

        @blk.gpsimd
        def _(eng):
            S.replay(GP, sems, eng)
            for i in range(NLANES_ST):
                nm = f"sST{i}"
                if S.val.get(nm, 0):
                    eng.wait_ge(sems[nm], S.val[nm])
    stack.close()
    return nc


_NC_CACHE = {}

def _get_program(n_tiles=16):
    if n_tiles not in _NC_CACHE:
        _NC_CACHE[n_tiles] = build_program(n_tiles)
    return _NC_CACHE[n_tiles]


def _fm_vec(vv):
    return np.ascontiguousarray(np.asarray(vv, np.float32).reshape(-1, 128).T)


def make_common(Wk, Wv, Wr, Wck, Wcv, Wcr, w1, w2, tdec, tfirst, tshiftv,
                g1, b1, g2, b2, fb1, fb2):
    return {
        "Wk": np.ascontiguousarray(Wk, np.float32),
        "Wv": np.ascontiguousarray(Wv, np.float32),
        "Wr": np.ascontiguousarray(Wr, np.float32),
        "Wck": np.ascontiguousarray(Wck, np.float32),
        "Wcv": np.ascontiguousarray(Wcv, np.float32),
        "Wcr": np.ascontiguousarray(Wcr, np.float32),
        "w1": np.ascontiguousarray(w1, np.float32),
        "w2": np.ascontiguousarray(w2, np.float32),
        "tdec": _fm_vec(tdec), "tfirst": _fm_vec(tfirst),
        "tshift": _fm_vec(np.asarray(tshiftv, np.float32).reshape(-1)),
        "ln1g": _fm_vec(g1), "ln1b": _fm_vec(b1),
        "ln2g": _fm_vec(g2), "ln2b": _fm_vec(b2),
        "b1v": _fm_vec(fb1), "b2v": _fm_vec(fb2),
        "ident": np.eye(128, dtype=np.float32),
        "onesc": np.ones((128, 1), np.float32),
        "onesr": np.ones((1, 128), np.float32),
        "nramp": np.ascontiguousarray(
            np.broadcast_to(-(np.arange(T, dtype=np.float32) + 1.0), (128, T))),
    }


def kernel(L0_attn_x, L0_attn_time_decay, L0_attn_time_first, L0_attn_time_shift,
           L0_attn_Wk, L0_attn_Wv, L0_attn_Wr, L0_attn_Wck, L0_attn_Wcv, L0_attn_Wcr,
           L0_attn_ln1_g, L0_attn_ln1_b, L0_attn_ln2_g, L0_attn_ln2_b,
           L0_ffn_w1, L0_ffn_b1, L0_ffn_w2, L0_ffn_b2):
    x = np.ascontiguousarray(np.asarray(L0_attn_x, dtype=np.float32))
    nc = _get_program(16)
    common = make_common(L0_attn_Wk, L0_attn_Wv, L0_attn_Wr, L0_attn_Wck, L0_attn_Wcv,
                         L0_attn_Wcr, L0_ffn_w1, L0_ffn_w2, L0_attn_time_decay,
                         L0_attn_time_first, L0_attn_time_shift, L0_attn_ln1_g,
                         L0_attn_ln1_b, L0_attn_ln2_g, L0_attn_ln2_b, L0_ffn_b1,
                         L0_ffn_b2)
    in_maps = [dict(common, x=np.ascontiguousarray(x[b])) for b in range(B)]
    res = run_bass_kernel_spmd(nc, in_maps, core_ids=list(range(B)))
    return np.stack([np.asarray(res.results[b]["y"]) for b in range(B)], axis=0)
